# version 38
# baseline (speedup 1.0000x reference)
import contextlib
import numpy as np
import concourse.bass as bass
import concourse.mybir as mybir
from concourse.bass_utils import run_bass_kernel_spmd

F32 = mybir.dt.float32
BF16 = mybir.dt.bfloat16
AF = mybir.ActivationFunctionType
ALU = mybir.AluOpType

THETA = (0.24419431388378143, 0.4639692008495331, 0.6861860156059265,
         0.9059608578681946, 1.083978533744812)

D = 1024
DC = 8
FF = 4096
FCN = 32
H = 16
DH = 64
NP = 8
EPS = 1e-5
NEGBIG = -1.0e30

ENGS = ("T", "V", "A", "S", "G")


class Prog:

    def __init__(self):
        self.streams = {e: [] for e in ENGS}
        self.sems = {}

    def inc(self, name, n=1):
        self.sems[name] = self.sems.get(name, 0) + n
        return self.sems[name]

    def val(self, name):
        return self.sems.get(name, 0)

    def emit(self, eng, fn, inc=None):
        if inc is not None:
            self.inc(inc[0], inc[1])

        def run(E, SEM, fn=fn, inc=inc):
            ins = fn(E)
            if inc is not None:
                ins.then_inc(SEM[inc[0]], inc[1])
        self.streams[eng].append(run)

    def wait(self, eng, name, val=None):
        v = self.val(name) if val is None else val
        if v <= 0:
            return
        self.streams[eng].append(
            lambda E, SEM, name=name, v=v: E.wait_ge(SEM[name], v))

    def drain(self, eng):
        self.streams[eng].append(lambda E, SEM: E.drain())


def build_program(S=2048, stage="full"):
    NT = S // 512
    NQB = S // 256
    nc = bass.Bass("TRN2", target_bir_lowering=False, debug=False)

    xin = nc.dram_tensor("xin", [D, S], F32, kind="ExternalInput").ap()
    wfcT = nc.dram_tensor("wfcT", [D, FF], F32, kind="ExternalInput").ap()
    wprojT = nc.dram_tensor("wprojT", [FF, D], F32, kind="ExternalInput").ap()
    wq = nc.dram_tensor("wq", [D, D], F32, kind="ExternalInput").ap()
    wkk = nc.dram_tensor("wkk", [D, 128], F32, kind="ExternalInput").ap()
    wvv = nc.dram_tensor("wvv", [D, DH], F32, kind="ExternalInput").ap()
    wo = nc.dram_tensor("wo", [D, D], F32, kind="ExternalInput").ap()
    gamd = nc.dram_tensor("gamd", [128, 8], F32, kind="ExternalInput").ap()
    trid = nc.dram_tensor("trid", [128, 128], F32, kind="ExternalInput").ap()
    identd = nc.dram_tensor("identd", [128, 128], F32, kind="ExternalInput").ap()
    e2d = nc.dram_tensor("e2d", [2, 128], F32, kind="ExternalInput").ap()
    xout = nc.dram_tensor("xout", [D, S], F32, kind="ExternalOutput").ap()
    mu_d = nc.dram_tensor("mu_d", [1, S], F32, kind="Internal").ap()
    m2_d = nc.dram_tensor("m2_d", [1, S], F32, kind="Internal").ap()
    r_d = nc.dram_tensor("r_d", [1, S], F32, kind="Internal").ap()
    t2_d = nc.dram_tensor("t2_d", [1, S], F32, kind="Internal").ap()
    nc_d = nc.dram_tensor("nc_d", [1, 8], F32, kind="Internal").ap()
    sums_d = nc.dram_tensor("sums_d", [16, 2048], F32, kind="Internal").ap()
    wp16 = nc.dram_tensor("wp16", [8 * 128, 4096], BF16, kind="Internal").ap()
    wf16 = nc.dram_tensor("wf16", [DC * 128, FF], BF16, kind="Internal").ap()

    X = nc.alloc_sbuf_tensor("X", [128, DC * S], F32)
    WB = nc.alloc_sbuf_tensor("WB", [128, 32768], BF16)
    HB = nc.alloc_sbuf_tensor("HB", [128, 16384], BF16)
    Q3 = nc.alloc_sbuf_tensor("Q3", [128, 16384], BF16)
    MU_t = nc.alloc_sbuf_tensor("MU_t", [128, 512], F32)
    R_t = nc.alloc_sbuf_tensor("R_t", [128, 512], F32)
    stg = nc.alloc_sbuf_tensor("stg", [1, 1024], F32)
    FJ = S // 128
    acc = nc.alloc_sbuf_tensor("acc", [128, 512], F32)
    ident = acc[:, 0:128]
    ysb = nc.alloc_sbuf_tensor("ysb", [128, 1024], F32)
    T2_t = ysb[:, 0:512]
    fold = ysb[:, 512:512 + 8 * FJ]
    sacc = nc.alloc_sbuf_tensor("sacc", [128, 128], F32)
    rs16 = nc.alloc_sbuf_tensor("rs16", [128, 16], F32)
    e2 = nc.alloc_sbuf_tensor("e2", [2, 128], BF16)
    ncb = nc.alloc_sbuf_tensor("ncb", [128, 8], F32)
    tri = nc.alloc_sbuf_tensor("tri", [128, 128], BF16)
    gam = nc.alloc_sbuf_tensor("gam", [128, 8], F32)
    ones = nc.alloc_sbuf_tensor("ones", [128, 2], BF16)
    crow = nc.alloc_sbuf_tensor("crow", [1, 64], F32)

    def foldv(i):
        return fold[:, i * FJ:(i + 1) * FJ]

    def xsl(dc, t0, n=512):
        return X[:, dc * S + t0: dc * S + t0 + n]

    def wfc_t(dc, fc):
        return WB[:, dc * FF + fc * 128: dc * FF + fc * 128 + 128]

    def h_t(fc, n=512):
        return HB[:, fc * 512: fc * 512 + n]

    def wpe_t(par, fcl, dc):
        base = par * 4096 + fcl * 1024 + dc * 128
        return Q3[:, base: base + 128]

    def wpe_full(par):
        return Q3[:, par * 4096:(par + 1) * 4096]

    def xn_t(dc):
        return Q3[:, 8192 + dc * 512: 8192 + dc * 512 + 512]

    def xb_t(par):
        return Q3[:, 12288 + par * 512: 12288 + par * 512 + 512]

    def xq_t(par):
        return Q3[:, 13312 + par * 512: 13312 + par * 512 + 512]

    def hh_t(par):
        return Q3[:, 14336 + par * 512: 14336 + par * 512 + 512]

    def qs_t(pc, t0, n=512, p0=0, pn=128):
        return Q3[p0:p0 + pn, pc * S + t0: pc * S + t0 + n]

    def a_t(dc):
        return HB[:, dc * 512: dc * 512 + 512]

    def outT_t(pc, q0, n):
        return HB[:, pc * S + q0: pc * S + q0 + n]

    WQP, WOP, P0P, P1P, KTP, VBP, VTP, WKP, WVP, MSP, R2P = (
        0, 8192, 8192, 16384, 24576, 26624, 27648, 30720, 31744, 27648, 29696)

    def wq_t(dc, oc):
        return WB[:, WQP + dc * 1024 + oc * 128: WQP + dc * 1024 + oc * 128 + 128]

    def wo_t(pc, dc):
        return WB[:, WOP + pc * 1024 + dc * 128: WOP + pc * 1024 + dc * 128 + 128]

    def kt2_t(t0, n=512, p0=0, pn=128):
        return WB[p0:p0 + pn, KTP + t0: KTP + t0 + n]

    def vb_t(jc):
        return WB[:, VBP + jc * 64: VBP + jc * 64 + 64]

    def p2_t(par, pc, hl, n=512):
        base = (P0P if par == 0 else P1P) + (pc * 2 + hl) * n
        return WB[:, base: base + n]

    def wkk_t(dc):
        return WB[:, WKP + dc * 128: WKP + dc * 128 + 128]

    def wvv_t(dc):
        return WB[:, WVP + dc * 64: WVP + dc * 64 + 64]

    def vt_t(t0, n):
        return WB[0:64, VTP + t0: VTP + t0 + n]

    def msc_t(c0, n):
        return WB[:, MSP + c0: MSP + c0 + n]

    def r2_t(c0, n):
        return WB[0:2, R2P + c0: R2P + c0 + n]

    def scr_t(n=512, p0=0, pn=128):
        return WB[p0:p0 + pn, VTP: VTP + n]

    PB = [nc.alloc_psum_tensor(f"pb{i}", [128, 512], F32) for i in range(8)]

    P = Prog()

    for dc in range(DC):
        P.emit("S", lambda E, dc=dc: E.dma_start(
            out=xsl(dc, 0, S), in_=xin[dc * 128:(dc + 1) * 128, :]),
            inc=("g_x", 16))
    for dc in range(DC):
        P.emit("G", lambda E, dc=dc: E.dma_start(
            out=WB[:, dc * FF: dc * FF + FF],
            in_=wfcT[dc * 128:(dc + 1) * 128, :]), inc=("g_wfc", 16))
    P.emit("G", lambda E: E.dma_start(out=gam[:], in_=gamd), inc=("g_misc", 16))
    P.emit("G", lambda E: E.dma_start(out=tri[:], in_=trid), inc=("g_tri", 16))

    P.emit("V", lambda E: E.memset(ones[:, 0:1], 1.0 / 1024.0))
    P.drain("V")
    P.emit("V", lambda E: E.memset(ones[:, 1:2], 1.0), inc=("v_on", 1))
    P.drain("V")

    def ln_stats(tag, with_t2, xwait=None, sgate=None, banks=(6, 7)):
        B0, B1 = banks
        P.wait("V", "g_x", 16 * DC)
        P.wait("A", "g_x", 16 * DC)
        P.wait("T", "v_on", 1)
        def xb4(p):
            return Q3[:, 12288 + p * 512: 12288 + p * 512 + 512]

        def xq4(p):
            return Q3[:, 14336 + p * 512: 14336 + p * 512 + 512]

        if sgate is not None:
            P.wait("A", "v_h")
        for tb in range(NT):
            if xwait is not None:
                P.wait("V", xwait[0], xwait[1](tb))
            for dc in range(DC):
                k = tb * DC + dc
                if sgate is not None:
                    par = k % 4
                    P.wait("G", "t_sx", max(0, P.val("t_sx") - 3))
                    P.wait("G", sgate[0], sgate[1](tb, dc))
                    P.emit("G", lambda E, dc=dc, tb=tb, par=par: E.dma_start(
                        out=xb4(par), in_=xsl(dc, tb * 512)),
                        inc=(f"g_xb{par}", 16))
                    P.wait("A", "t_sx", max(0, P.val("t_sx") - 3))
                    P.wait("A", sgate[0], sgate[1](tb, dc))
                    P.emit("A", lambda E, par=par, dc=dc, tb=tb: E.activation(
                        xq4(par), xsl(dc, tb * 512), AF.Square),
                        inc=("a_sq", 1))
                    P.wait("T", f"g_xb{par}")
                    P.emit("T", lambda E, par=par, B0=B0, dc=dc: E.matmul(
                        PB[B0][0:1, :], ones[:, 0:1], xb4(par),
                        start=(dc == 0), stop=(dc == DC - 1)))
                    P.wait("T", "a_sq")
                    P.emit("T", lambda E, par=par, B1=B1, dc=dc: E.matmul(
                        PB[B1][0:1, :], ones[:, 0:1], xq4(par),
                        start=(dc == 0), stop=(dc == DC - 1)),
                        inc=("t_sx", 1))
                    continue
                par = dc % 2
                P.wait("V", "t_sx", max(0, P.val("t_sx") - 1))
                P.emit("V", lambda E, dc=dc, tb=tb, par=par: E.tensor_copy(
                    xb_t(par)[:], xsl(dc, tb * 512)), inc=("v_xb", 1))
                P.wait("A", "t_sx", max(0, P.val("t_sx") - 1))
                if xwait is not None:
                    P.wait("A", xwait[0], xwait[1](tb))
                P.emit("A", lambda E, par=par, dc=dc, tb=tb: E.activation(
                    xq_t(par)[:], xsl(dc, tb * 512), AF.Square), inc=("a_sq", 1))
                P.wait("T", "v_xb")
                P.emit("T", lambda E, par=par, dc=dc, B0=B0: E.matmul(
                    PB[B0][0:1, :], ones[:, 0:1], xb_t(par)[:],
                    start=(dc == 0), stop=(dc == DC - 1)))
                P.wait("T", "a_sq")
                P.emit("T", lambda E, par=par, dc=dc, B1=B1: E.matmul(
                    PB[B1][0:1, :], ones[:, 0:1], xq_t(par)[:],
                    start=(dc == 0), stop=(dc == DC - 1)),
                    inc=("t_sx", 1))
            P.wait("V", "t_sx")
            P.wait("V", "s_sg")
            P.emit("V", lambda E, B0=B0: E.tensor_copy(
                stg[0:1, 0:512], PB[B0][0:1, :]))
            P.drain("V")
            P.emit("V", lambda E, B1=B1: E.tensor_copy(
                stg[0:1, 512:1024], PB[B1][0:1, :]), inc=("v_sg", 1))
            P.wait("S", "v_sg")
            P.emit("S", lambda E, tb=tb: E.dma_start(
                out=mu_d[0:1, tb * 512:tb * 512 + 512], in_=stg[0:1, 0:512]),
                inc=("s_sg", 16))
            P.emit("S", lambda E, tb=tb: E.dma_start(
                out=m2_d[0:1, tb * 512:tb * 512 + 512], in_=stg[0:1, 512:1024]),
                inc=("s_sg", 16))
            P.wait("T", "v_sg")

        P.wait("S", "s_sg")
        P.emit("S", lambda E: E.dma_start(
            out=foldv(0), in_=mu_d.rearrange("o (p n) -> (o p) n", p=128)),
            inc=("s_fl", 16))
        P.emit("S", lambda E: E.dma_start(
            out=foldv(1), in_=m2_d.rearrange("o (p n) -> (o p) n", p=128)),
            inc=("s_fl", 16))
        P.wait("V", "s_fl")
        P.emit("V", lambda E: E.tensor_mul(foldv(2), foldv(0), foldv(0)))
        P.drain("V")
        P.emit("V", lambda E: E.tensor_sub(foldv(2), foldv(1), foldv(2)))
        P.drain("V")
        P.emit("V", lambda E: E.tensor_scalar_add(foldv(3), foldv(2), EPS),
               inc=("v_u", 1))
        P.wait("A", "v_u")
        P.emit("A", lambda E: E.activation(foldv(4), foldv(3), AF.Sqrt),
               inc=("a_s0", 1))
        P.wait("V", "a_s0")
        P.emit("V", lambda E: E.reciprocal(foldv(5), foldv(4)))
        P.drain("V")
        P.emit("V", lambda E: E.tensor_mul(foldv(6), foldv(5), foldv(5)))
        P.drain("V")
        P.emit("V", lambda E: E.tensor_mul(foldv(6), foldv(6), foldv(3)))
        P.drain("V")
        P.emit("V", lambda E: E.tensor_scalar(foldv(6), foldv(6), -0.5, 1.5,
                                              ALU.mult, ALU.add))
        P.drain("V")
        P.emit("V", lambda E: E.tensor_mul(foldv(6), foldv(6), foldv(5)),
               inc=("v_r", 1))
        P.wait("S", "v_r")
        P.emit("S", lambda E: E.dma_start(
            out=r_d.rearrange("o (p n) -> (o p) n", p=128), in_=foldv(6)),
            inc=("s_r", 16))
        if with_t2:
            P.drain("V")
            P.emit("V", lambda E: E.reciprocal(foldv(2), foldv(3)))
            P.drain("V")
            P.emit("V", lambda E: E.tensor_scalar(foldv(2), foldv(2), -EPS,
                                                  1.0 + EPS, ALU.mult, ALU.add),
                   inc=("v_w", 1))
            P.wait("A", "v_w")
            P.emit("A", lambda E: E.activation(foldv(4), foldv(2), AF.Sqrt),
                   inc=("a_s1", 1))
            P.wait("V", "a_s1")
            P.emit("V", lambda E: E.reciprocal(foldv(5), foldv(4)))
            P.drain("V")
            P.emit("V", lambda E: E.tensor_mul(foldv(7), foldv(5), foldv(5)))
            P.drain("V")
            P.emit("V", lambda E: E.tensor_mul(foldv(7), foldv(7), foldv(2)))
            P.drain("V")
            P.emit("V", lambda E: E.tensor_scalar(foldv(7), foldv(7), -0.5, 1.5,
                                                  ALU.mult, ALU.add))
            P.drain("V")
            P.emit("V", lambda E: E.tensor_mul(foldv(7), foldv(7), foldv(5)))
            P.drain("V")
            P.emit("V", lambda E: E.tensor_scalar_mul(foldv(7), foldv(7), 0.125))
            P.drain("V")
            P.emit("V", lambda E: E.tensor_mul(foldv(7), foldv(7), foldv(6)),
                   inc=("v_t2", 1))
            P.wait("S", "v_t2")
            P.emit("S", lambda E: E.dma_start(
                out=t2_d.rearrange("o (p n) -> (o p) n", p=128), in_=foldv(7)),
                inc=("s_t2", 16))

    def ffn(tag, store_out=False):
        ln_stats(tag, with_t2=False,
                 xwait=("v_wo", lambda tb: (tb + 1) * DC) if tag == "f2" else None)
        tg0 = P.val("t_g")
        tye0 = P.val("t_ye")
        tyd0 = P.val("t_yd")
        vyc0 = P.val("v_yc")
        vsp0 = P.val("v_sp")

        def emit_spike(tb, dc):
            if dc == 0:
                P.wait("V", "t_ye", tye0 + (tb + 1) * DC)
            ysrc = (lambda: ysb[:, dc * 512:(dc + 1) * 512]) if dc < 2 \
                else (lambda: PB[dc][:])
            P.emit("V", lambda E, dc=dc: E.tensor_single_scalar(
                acc[:], ysrc(), THETA[0], ALU.is_ge))
            P.drain("V")
            for i in range(1, 5):
                P.emit("V", lambda E, dc=dc, i=i: E.scalar_tensor_tensor(
                    acc[:], ysrc(), THETA[i], acc[:], ALU.is_ge, ALU.add))
                P.drain("V")
            P.emit("V", lambda E, dc=dc, tb=tb: E.scalar_tensor_tensor(
                xsl(dc, tb * 512), acc[:], 0.1, xsl(dc, tb * 512),
                ALU.mult, ALU.add), inc=("v_sp", 1))
            P.drain("V")
            if store_out:
                P.wait("S", "v_sp")
                P.emit("S", lambda E, dc=dc, tb=tb: E.dma_start(
                    out=xout[dc * 128:(dc + 1) * 128, tb * 512:(tb + 1) * 512],
                    in_=xsl(dc, tb * 512)), inc=("s_out", 16))

        def emit_apply(tb):
            P.wait("S", "v_ap")
            P.emit("S", lambda E, tb=tb: E.dma_start(
                out=MU_t[:], in_=bass.AP(mu_d.tensor, tb * 512, [[0, 128], [1, 512]])),
                inc=("s_mu", 16))
            P.wait("S", "s_r")
            P.emit("S", lambda E, tb=tb: E.dma_start(
                out=R_t[:], in_=bass.AP(r_d.tensor, tb * 512, [[0, 128], [1, 512]])),
                inc=("s_mu", 16))
            P.wait("V", "s_mu")
            P.wait("V", "t_g", tg0 + tb * FCN)
            for dc in range(DC):
                P.emit("V", lambda E, dc=dc, tb=tb: E.tensor_sub(
                    acc[:], xsl(dc, tb * 512), MU_t[:]))
                P.drain("V")
                P.emit("V", lambda E, dc=dc: E.tensor_mul(
                    xn_t(dc)[:], acc[:], R_t[:]), inc=("v_ap", 1))
                P.drain("V")

        emit_apply(0)
        for tb in range(NT):
            if tag == "f1" and tb == 1:
                P.wait("A", "g_wfc", 16 * DC)
                for dc in range(DC):
                    P.emit("A", lambda E, dc=dc: E.dma_start(
                        out=wf16[dc * 128:(dc + 1) * 128, :],
                        in_=WB[:, dc * FF: dc * FF + FF]), inc=("s_wf16", 16))
            vap_tb = P.val("v_ap") - DC
            P.wait("T", "v_yc", vyc0 + tb)
            if tag == "f1":
                P.wait("T", "g_wfc", 16 * DC)
            else:
                P.wait("T", "g_wfc2", 16 * DC)
            for fc in range(FCN):
                par = fc % 2
                P.wait("T", "a_rl", max(0, P.val("a_rl") - 1))
                for dc in range(DC):
                    if fc == 0:
                        P.wait("T", "v_ap", vap_tb + dc + 1)
                    P.emit("T", lambda E, par=par, dc=dc, fc=fc: E.matmul(
                        PB[par][:], wfc_t(dc, fc), xn_t(dc)[:],
                        start=(dc == 0), stop=(dc == DC - 1)),
                        inc=("t_g", 1) if dc == DC - 1 else None)
                P.wait("A", "t_g")
                P.wait("A", "v_h", max(0, P.val("v_h") - 1))
                P.emit("A", lambda E, par=par: E.activation(
                    hh_t(par)[:], PB[par][:], AF.Relu), inc=("a_rl", 1))
                P.wait("V", "a_rl")
                P.emit("V", lambda E, par=par, fc=fc: E.tensor_mul(
                    h_t(fc), hh_t(par)[:], hh_t(par)[:]), inc=("v_h", 1))
                if tb > 0 and fc % 4 == 3:
                    emit_spike(tb - 1, fc // 4)
            for e in range(8):
                par = e % 2
                ge = tb * 8 + e
                P.wait("G", "t_ye", max(tye0, tye0 + ge - 1))
                if tag == "f1" and tb == 0 and e >= 2:
                    P.wait("G", f"s_wp16p{e % 2}", 16 * (e // 2))
                if tag == "f2" and tb == 0 and e < 2:
                    P.wait("G", "t_s")
                if tag == "f1" and tb == 0:
                    for fl in range(4):
                        P.emit("G", lambda E, e=e, par=par, fl=fl: E.dma_start(
                            out=Q3[:, par * 4096 + fl * 1024: par * 4096 + fl * 1024 + 1024],
                            in_=wprojT[e * 512 + fl * 128: e * 512 + fl * 128 + 128, :]),
                            inc=(f"g_wp{par}", 16))
                    P.wait("S", f"g_wp{par}")
                    P.emit("S", lambda E, e=e, par=par: E.dma_start(
                        out=wp16[e * 128:(e + 1) * 128, :], in_=wpe_full(par)),
                        inc=(f"s_wp16p{par}", 16))
                else:
                    if tag == "f1" and tb == 1 and e < 2:
                        P.wait("G", f"s_wp16p{e % 2}", 64)
                    P.emit("G", lambda E, e=e, par=par: E.dma_start(
                        out=wpe_full(par), in_=wp16[e * 128:(e + 1) * 128, :]),
                        inc=(f"g_wp{par}", 16))
                P.wait("T", f"g_wp{par}")
                if e == 0:
                    P.wait("T", "v_h")
                for dc in range(DC):
                    if e == 0 and tb > 0:
                        P.wait("T", "v_sp", vsp0 + (tb - 1) * DC + dc + 1)
                    for fcl in range(4):
                        fc = e * 4 + fcl
                        st = (e == 0 and fcl == 0)
                        sp = (e == 7 and fcl == 3)
                        if dc == DC - 1 and fcl == 3:
                            mark = ("t_ye", 1)
                        elif e == 7 and fcl == 3 and dc < 2:
                            mark = ("t_yd", 1)
                        else:
                            mark = None
                        P.emit("T", lambda E, dc=dc, par=par, fcl=fcl, fc=fc,
                               st=st, sp=sp: E.matmul(
                            PB[dc][:], wpe_t(par, fcl, dc), h_t(fc),
                            start=st, stop=sp), inc=mark)
            if tb + 1 < NT:
                emit_apply(tb + 1)
            P.wait("V", "t_yd", tyd0 + (tb + 1) * 2)
            P.emit("V", lambda E: E.tensor_copy(ysb[:, 0:512], PB[0][:]))
            P.drain("V")
            P.emit("V", lambda E: E.tensor_copy(ysb[:, 512:1024], PB[1][:]),
                   inc=("v_yc", 1))
            P.drain("V")
            if tb == NT - 1:
                for dc in range(DC):
                    emit_spike(tb, dc)

    def attention():
        vsp_base = P.val("v_sp") - NT * DC
        vspL = vsp_base + (NT - 1) * DC
        P.wait("G", "t_g")
        P.wait("G", "t_ye")
        P.wait("G", "s_wf16", 16 * DC)
        for dc in range(DC):
            P.emit("G", lambda E, dc=dc: E.dma_start(
                out=WB[:, WKP + dc * 128: WKP + dc * 128 + 128],
                in_=wkk[dc * 128:(dc + 1) * 128, :]), inc=("g_wk", 16))
            P.emit("G", lambda E, dc=dc: E.dma_start(
                out=WB[:, WVP + dc * 64: WVP + dc * 64 + 64],
                in_=wvv[dc * 128:(dc + 1) * 128, :]), inc=("g_wk", 16))
        for dc in range(DC):
            P.emit("G", lambda E, dc=dc: E.dma_start(
                out=WB[:, WQP + dc * 1024: WQP + dc * 1024 + 1024],
                in_=wq[dc * 128:(dc + 1) * 128, :]), inc=("g_wq", 16))
        P.wait("T", "v_sp", vspL + 2)
        ln_stats("a", with_t2=True,
                 sgate=("v_sp", lambda tb, dc: vsp_base + tb * DC + dc + 1),
                 banks=(0, 1))
        P.wait("V", "g_wq", 16 * DC)
        P.wait("V", "g_misc", 16)
        for dc in range(DC):
            P.emit("V", lambda E, dc=dc: E.tensor_scalar_mul(
                WB[:, WQP + dc * 1024: WQP + dc * 1024 + 1024],
                WB[:, WQP + dc * 1024: WQP + dc * 1024 + 1024],
                gam[:, dc:dc + 1]), inc=("v_gw", 1))
            P.drain("V")

        P.wait("V", "s_wp16p0", P.val("s_wp16p0"))
        P.wait("V", "s_wp16p1", P.val("s_wp16p1"))
        vapa0 = P.val("v_apa")
        for tb in range(NT):
            P.wait("S", "v_apa")
            P.wait("S", "v_qs")
            P.emit("S", lambda E, tb=tb: E.dma_start(
                out=MU_t[:], in_=bass.AP(mu_d.tensor, tb * 512, [[0, 128], [1, 512]])),
                inc=("s_muo", 16))
            P.wait("S", "s_r")
            P.emit("S", lambda E, tb=tb: E.dma_start(
                out=R_t[:], in_=bass.AP(r_d.tensor, tb * 512, [[0, 128], [1, 512]])),
                inc=("s_mua", 16))
            P.wait("S", "s_t2")
            P.emit("S", lambda E, tb=tb: E.dma_start(
                out=T2_t, in_=bass.AP(t2_d.tensor, tb * 512, [[0, 128], [1, 512]])),
                inc=("s_mua", 16))
            P.wait("V", "s_muo", 16 * (tb + 1))
            P.wait("V", "v_gw", DC)
            P.wait("V", "t_kv", tb)
            P.wait("V", "t_vt", tb)
            P.wait("V", "t_qs", tb * DC)
            for dc in range(DC):
                P.emit("V", lambda E, dc=dc, tb=tb: E.tensor_sub(
                    a_t(dc)[:], xsl(dc, tb * 512), MU_t[:]), inc=("v_apa", 1))
                P.drain("V")
            P.wait("T", "g_wk", 16 * 2 * DC)
            P.wait("T", "v_kv", tb)
            for dc in range(DC):
                P.wait("T", "v_apa", vapa0 + tb * DC + dc + 1)
                P.emit("T", lambda E, dc=dc: E.matmul(
                    PB[3][:], wkk_t(dc), a_t(dc)[:],
                    start=(dc == 0), stop=(dc == DC - 1)),
                    inc=("t_kv", 1) if dc == DC - 1 else None)
            P.wait("T", "v_vt", tb)
            for dc in range(DC):
                P.emit("T", lambda E, dc=dc: E.matmul(
                    PB[4][0:64, :], wvv_t(dc), a_t(dc)[:],
                    start=(dc == 0), stop=(dc == DC - 1)),
                    inc=("t_vt", 1) if dc == DC - 1 else None)
            P.wait("V", "s_mua")
            P.wait("V", "t_kv")
            P.emit("V", lambda E, tb=tb: E.tensor_mul(
                kt2_t(tb * 512), PB[3][:], R_t[:]), inc=("v_kv", 1))
            P.drain("V")
            P.wait("V", "t_vt")
            P.emit("V", lambda E, tb=tb: E.tensor_mul(
                vt_t(tb * 512, 512), PB[4][0:64, :], R_t[0:64, :]),
                inc=("v_vt", 1))
            P.drain("V")
            for oc in range(DC):
                bank = 5 + (oc % 2)
                P.wait("T", "v_qs", max(0, P.val("v_qs") - 1))
                for dc in range(DC):
                    P.emit("T", lambda E, oc=oc, dc=dc, bank=bank: E.matmul(
                        PB[bank][:], wq_t(dc, oc), a_t(dc)[:],
                        start=(dc == 0), stop=(dc == DC - 1)),
                        inc=("t_qs", 1) if dc == DC - 1 else None)
                P.wait("V", "t_qs")
                P.emit("V", lambda E, oc=oc, tb=tb, bank=bank: E.tensor_mul(
                    qs_t(oc, tb * 512), PB[bank][:], T2_t), inc=("v_qs", 1))
                P.drain("V")
                P.emit("V", lambda E, oc=oc, tb=tb: E.tensor_mul(
                    WB[:, 8192:8704], qs_t(oc, tb * 512), qs_t(oc, tb * 512)),
                    inc=("v_qn2", 1))
                P.drain("V")
                P.wait("T", "v_qn2")
                P.wait("T", "v_qn")
                P.emit("T", lambda E: E.matmul(
                    PB[7][0:1, :], ones[:, 1:2], WB[:, 8192:8704],
                    start=True, stop=True), inc=("t_qn", 1))
                P.wait("V", "t_qn")
                P.emit("V", lambda E, oc=oc, tb=tb: E.reduce_max(
                    crow[0:1, oc * 4 + tb: oc * 4 + tb + 1], PB[7][0:1, :],
                    axis=mybir.AxisListType.X), inc=("v_qn", 1))
                P.drain("V")

        P.wait("S", "v_vt")
        P.emit("S", lambda E: E.dma_start_transpose(
            WB[:, VBP:VBP + S // 2].rearrange("p (c n) -> p c n", n=64),
            WB[0:64, VTP:VTP + S].rearrange("p (c n) -> p c n", n=128)),
            inc=("s_vb", 16))

        P.wait("A", "t_qn")
        P.wait("A", "s_vb", 16)
        P.wait("T", "v_qn")
        for tt in range(NT):
            P.wait("A", "v_kv")
            P.wait("A", "t_kn")
            P.emit("A", lambda E, tt=tt: E.activation(
                scr_t(512, 0, 64), kt2_t(tt * 512, 512, 0, 64), AF.Square),
                inc=("a_kn", 1))
            P.wait("T", "a_kn")
            P.wait("T", "v_kn")
            P.emit("T", lambda E: E.matmul(
                PB[7][0:1, :], ones[0:64, 1:2], scr_t(512, 0, 64),
                start=True, stop=True), inc=("t_kn", 1))
            P.wait("V", "t_kn")
            P.emit("V", lambda E, tt=tt: E.reduce_max(
                crow[0:1, 32 + tt: 33 + tt], PB[7][0:1, :],
                axis=mybir.AxisListType.X), inc=("v_kn", 1))
        P.wait("V", "v_qn")
        P.wait("V", "v_kn")
        for pc in range(NP):
            P.emit("V", lambda E, pc=pc: E.reduce_max(
                crow[0:1, 40 + pc: 41 + pc], crow[0:1, pc * 4: pc * 4 + NT],
                axis=mybir.AxisListType.X))
            P.drain("V")
        P.emit("V", lambda E: E.reduce_max(
            crow[0:1, 48:49], crow[0:1, 32: 32 + NT], axis=mybir.AxisListType.X))
        P.drain("V")
        P.emit("V", lambda E: E.tensor_single_scalar(
            crow[0:1, 40:48], crow[0:1, 40:48], crow[0:1, 48:49], ALU.mult),
            inc=("v_c1", 1))
        P.wait("A", "v_c1")
        P.emit("A", lambda E: E.activation(
            crow[0:1, 56:64], crow[0:1, 40:48], AF.Sqrt), inc=("a_c", 1))
        P.wait("V", "a_c")
        P.emit("V", lambda E: E.tensor_scalar_mul(
            crow[0:1, 56:64], crow[0:1, 56:64], -1.0), inc=("v_c2", 1))
        P.wait("S", "v_c2")
        P.emit("S", lambda E: E.dma_start(out=nc_d, in_=crow[0:1, 56:64]),
               inc=("s_c", 16))
        P.wait("S", "s_c")
        P.emit("S", lambda E: E.dma_start(
            out=ncb[:], in_=bass.AP(nc_d.tensor, 0, [[0, 128], [1, 8]])),
            inc=("s_c", 16))

        P.wait("T", "s_vb", 16)
        P.wait("T", "v_kv")
        P.wait("T", "v_vt")
        P.wait("T", "v_qs")
        P.wait("A", "s_c", 32)
        P.wait("V", "g_tri", 16)
        P.wait("S", "v_gw", DC)
        P.emit("G", lambda E: E.dma_start(out=e2[:], in_=e2d), inc=("g_e2", 16))
        P.wait("T", "g_e2", 16)

        NQT = S // 128
        tr_done = {}
        ex_done = {}
        unit_of = {}
        uidx = 0
        pend_av = None
        pend_msc = None

        def emit_av(qt, jb, u):
            ext = (qt + 1) * 128
            jbn = (ext + 511) // 512
            N = min(512, ext - jb * 512)
            jcn = N // 128
            ob = 4 if qt % 2 == 0 else 6
            obs = 6 if qt % 2 == 0 else 4
            P.wait("T", "s_tr", tr_done[u])
            P.wait("T", "t_av", u)
            if jb == 0:
                P.wait("T", "v_msc", max(0, 2 * qt))
                P.wait("T", "v_mc", max(0, 2 * qt))
            for jcl in range(jcn):
                jc = jb * 4 + jcl
                st = (jb == 0 and jcl == 0)
                sp = (jb == jbn - 1 and jcl == jcn - 1)
                for hl in range(2):
                    for half in range(2):
                        lastmm = (jcl == jcn - 1 and hl == 1 and half == 1)
                        off = ((half * 8 + hl) * jcn + jcl) * 128
                        rhs = bass.AP(WB, off, [[WB.shape[1], 128],
                                                [2 * jcn * 128, 4], [1, 128]])
                        P.emit("T", lambda E, jc=jc, rhs=rhs, hl=hl, half=half,
                               st=st, sp=sp, ob=ob: E.matmul(
                                   PB[ob + half][hl * 64:(hl + 1) * 64, 0:512],
                                   vb_t(jc), rhs,
                                   start=st, stop=sp, skip_group_check=True,
                                   tile_position=(0, hl * 64)))
                        c = (half * 2 + hl) * 32
                        P.emit("T", lambda E, rhs=rhs, c=c, obs=obs,
                               st=st, sp=sp: E.matmul(
                                   PB[obs][c:c + 1, 0:512],
                                   ones[:, 1:2], rhs,
                                   start=st, stop=sp, skip_group_check=True,
                                   tile_position=(0, c)),
                               inc=("t_av", 1) if lastmm else None)

        def emit_msc(qt):
            ob = 4 if qt % 2 == 0 else 6
            obs = 6 if qt % 2 == 0 else 4
            lastu = unit_of[(qt, ((qt + 1) * 128 + 511) // 512 - 1)] + 1
            P.wait("V", "t_av", lastu)
            P.wait("V", "g_ms1", 16 * qt)
            for ci, c in enumerate((0, 32, 64, 96)):
                P.emit("V", lambda E, c=c, obs=obs: E.tensor_copy(
                    acc[c:c + 1, 0:512], PB[obs][c:c + 1, 0:512]),
                    inc=("v_sc", 1) if ci == 3 else None)
                P.drain("V")
            P.wait("G", "v_sc", qt + 1)
            P.emit("G", lambda E, qt=qt: E.dma_start(
                out=sums_d[qt:qt + 1, :],
                in_=bass.AP(acc, 0, [[32 * 512, 4], [1, 512]])),
                inc=("g_ms1", 16))
            P.wait("G", "g_ms1", 16 * (qt + 1))
            P.wait("G", "t_msm", qt)
            P.emit("G", lambda E, qt=qt: E.dma_start(
                out=r2_t(0, 1024),
                in_=bass.AP(sums_d.tensor, qt * 2048,
                            [[512, 2], [1024, 2], [1, 512]])),
                inc=("g_ms2", 16))
            P.wait("T", "g_ms2", 16 * (qt + 1))
            P.wait("T", "v_sc", qt + 1)
            for half in range(2):
                P.emit("T", lambda E, half=half, obs=obs: E.matmul(
                    PB[obs + half][:, 0:512], e2[:], r2_t(half * 512, 512),
                    start=True, stop=True, skip_group_check=True),
                    inc=("t_msm", 1) if half else None)
            P.wait("V", "t_msm", qt + 1)

            def recip_msc(E, half, obs):
                with nc.allow_low_precision(reason="elementwise 1/s tile"):
                    return E.reciprocal(msc_t(half * 512, 512),
                                        PB[obs + half][:, 0:512])
            for half in range(2):
                P.emit("V", lambda E, half=half, obs=obs:
                       recip_msc(E, half, obs), inc=("v_mc", 1))
                P.drain("V")
            for half in range(2):
                dstbase = half * 4 * S + qt * 128
                dst3 = bass.AP(HB, dstbase, [[HB.shape[1], 128], [S, 4], [1, 128]])
                P.emit("V", lambda E, half=half, ob=ob, dst3=dst3: E.tensor_tensor(
                    dst3,
                    PB[ob + half][:, 0:512].rearrange("p (pc n) -> p pc n", n=128),
                    msc_t(half * 512, 512).rearrange("p (pc n) -> p pc n", n=128),
                    op=ALU.mult), inc=("v_msc", 1))
                P.drain("V")

        for qt in range(NQT):
            ext = (qt + 1) * 128
            jbn = (ext + 511) // 512
            for jb in range(jbn):
                u = uidx
                uidx += 1
                unit_of[(qt, jb)] = u
                upar = u % 2
                N = min(512, ext - jb * 512)
                for pc in range(NP):
                    i = P.val("t_s") // 2
                    bA = (i % 2) * 2
                    bB = bA + 1
                    P.wait("T", "a_ex", max(0, P.val("a_ex") - 2))
                    P.emit("T", lambda E, pc=pc, qt=qt, jb=jb, N=N, bA=bA:
                           E.matmul(PB[bA][:, 0:N],
                                    qs_t(pc, qt * 128, 128, 0, 64),
                                    kt2_t(jb * 512, N, 0, 64),
                                    start=True, stop=True))
                    P.emit("T", lambda E, pc=pc, qt=qt, jb=jb, N=N, bB=bB:
                           E.matmul(PB[bB][:, 0:N],
                                    qs_t(pc, qt * 128, 128, 64, 64),
                                    kt2_t(jb * 512, N, 64, 64),
                                    start=True, stop=True), inc=("t_s", 2))
                    diag = qt * 128 - jb * 512
                    if 0 <= diag < N:
                        P.wait("V", "t_s")
                        for b in (bA, bB):
                            P.emit("V", lambda E, b=b, diag=diag: E.tensor_add(
                                PB[b][:, diag:diag + 128],
                                PB[b][:, diag:diag + 128], tri[:]),
                                inc=("v_msk", 1))
                            P.drain("V")
                    if 0 <= diag < N:
                        P.wait("A", "v_msk")
                    P.wait("A", "t_s")
                    if u >= 2 and pc == 0:
                        P.wait("A", "s_tr", tr_done[u - 2])
                    for hl in range(2):
                        P.emit("A", lambda E, hl=hl, par=upar, N=N,
                               b=(bA, bB)[hl], pc=pc: E.activation(
                                   p2_t(par, pc, hl, N), PB[b][:, 0:N],
                                   AF.Exp, bias=ncb[:, pc:pc + 1], scale=1.0),
                               inc=("a_ex", 1))
                P.wait("S", "a_ex")
                P.wait("S", "t_av", u)
                jcn = N // 128
                srcbase = P0P if upar == 0 else P1P
                src2 = bass.AP(WB, srcbase, [[WB.shape[1], 128], [1, 16 * N]])
                dst3 = bass.AP(WB, 0,
                               [[WB.shape[1], 128], [128, 16 * jcn], [1, 128]])
                P.emit("S", lambda E, dst3=dst3, src2=src2:
                       E.dma_start_transpose(dst3, src2), inc=("s_tr", 16))
                tr_done[u] = P.val("s_tr")
                if pend_av is not None:
                    pqt, pjb, pu = pend_av
                    emit_av(pqt, pjb, pu)
                    pext = (pqt + 1) * 128
                    if pjb == (pext + 511) // 512 - 1:
                        emit_msc(pqt)
                pend_av = (qt, jb, u)
        pqt, pjb, pu = pend_av
        emit_av(pqt, pjb, pu)
        emit_msc(pqt)

        P.wait("G", "a_ex")
        P.wait("G", "s_tr")
        for pc in range(NP):
            P.emit("G", lambda E, pc=pc: E.dma_start(
                out=WB[:, WOP + pc * 1024: WOP + pc * 1024 + 1024],
                in_=wo[pc * 128:(pc + 1) * 128, :]), inc=("g_wo", 16))
        if stage == "attn":
            otd = nc.dram_tensor("otd", [128, 16384], BF16, kind="ExternalOutput").ap()
            P.wait("S", "v_msc")
            P.emit("S", lambda E: E.dma_start(out=otd, in_=HB[:]), inc=("s_out", 16))
        P.wait("T", "v_msc")
        P.wait("T", "g_wo", 16 * NP)
        for tt in range(NT):
            for dc in range(DC):
                bank = (tt * DC + dc) % 2
                P.wait("T", "v_wo", max(0, P.val("v_wo") - 1))
                for pc in range(NP):
                    P.emit("T", lambda E, pc=pc, dc=dc, tt=tt, bank=bank: E.matmul(
                        PB[bank][:], wo_t(pc, dc), outT_t(pc, tt * 512, 512),
                        start=(pc == 0), stop=(pc == NP - 1)),
                        inc=("t_wo", 1) if pc == NP - 1 else None)
                P.wait("V", "t_wo")
                P.emit("V", lambda E, dc=dc, tt=tt, bank=bank: E.tensor_add(
                    xsl(dc, tt * 512), PB[bank][:], xsl(dc, tt * 512)),
                    inc=("v_wo", 1))
                P.drain("V")

        P.wait("G", "t_av")
        for dc in (0, 1, 4, 5):
            P.emit("G", lambda E, dc=dc: E.dma_start(
                out=WB[:, dc * FF: dc * FF + FF], in_=wf16[dc * 128:(dc + 1) * 128, :]),
                inc=("g_wfc2", 16))
        P.wait("G", "v_msc")
        P.wait("G", "t_msm")
        for dc in (6, 7):
            P.emit("G", lambda E, dc=dc: E.dma_start(
                out=WB[:, dc * FF: dc * FF + FF], in_=wf16[dc * 128:(dc + 1) * 128, :]),
                inc=("g_wfc2", 16))
        P.wait("G", "t_wo")
        for dc in (2, 3):
            P.emit("G", lambda E, dc=dc: E.dma_start(
                out=WB[:, dc * FF: dc * FF + FF], in_=wf16[dc * 128:(dc + 1) * 128, :]),
                inc=("g_wfc2", 16))


    ffn("f1", store_out=(stage == "ffn1"))
    if stage != "ffn1":
        attention()
        if stage == "attn":
            P.wait("S", "v_wo")
            for dc in range(DC):
                P.emit("S", lambda E, dc=dc: E.dma_start(
                    out=xout[dc * 128:(dc + 1) * 128, :], in_=xsl(dc, 0, S)),
                    inc=("s_out", 16))
        else:
            for e in ("V", "A", "T", "G"):
                P.wait(e, "t_s")
                P.wait(e, "t_av")
                P.wait(e, "a_ex")
                P.wait(e, "t_mst")
            ffn("f2", store_out=True)
    P.wait("S", "s_out")
    for name in list(P.sems):
        if name.startswith("s_"):
            P.wait("S", name)
        elif name.startswith("g_"):
            P.wait("G", name)

    semnames = sorted(P.sems.keys())
    with contextlib.ExitStack() as stack:
        sems = {n: stack.enter_context(nc.semaphore(n)) for n in semnames}
        with nc.Block() as block:
            @block.tensor
            def _(E):
                for fn in P.streams["T"]:
                    fn(E, sems)

            @block.vector
            def _(E):
                for fn in P.streams["V"]:
                    fn(E, sems)

            @block.scalar
            def _(E):
                for fn in P.streams["A"]:
                    fn(E, sems)

            @block.sync
            def _(E):
                for fn in P.streams["S"]:
                    fn(E, sems)

            @block.gpsimd
            def _(E):
                for fn in P.streams["G"]:
                    fn(E, sems)
    return nc


_cache = {}


def _get_program(S=2048, stage="full"):
    key = (S, stage)
    if key not in _cache:
        _cache[key] = build_program(S, stage)
    return _cache[key]


def make_inputs(L0_attn_x, L0_attn_gamma, L0_attn_Wq, L0_attn_Wkv, L0_attn_Wo,
                L0_ffn_W_fc, L0_ffn_W_proj):
    B = L0_attn_x.shape[0]
    wfcT_h = np.ascontiguousarray(np.asarray(L0_ffn_W_fc, np.float32).T)
    wprojT_h = np.ascontiguousarray(np.asarray(L0_ffn_W_proj, np.float32).T)
    wq_h = np.ascontiguousarray(np.asarray(L0_attn_Wq, np.float32))
    wkv = np.asarray(L0_attn_Wkv, np.float32)
    wkk_h = np.ascontiguousarray(np.concatenate([wkv[:, 0:DH], wkv[:, 0:DH]], axis=1))
    wvv_h = np.ascontiguousarray(wkv[:, DH:2 * DH])
    wo_h = np.ascontiguousarray(np.asarray(L0_attn_Wo, np.float32))
    gam_h = np.ascontiguousarray(
        np.asarray(L0_attn_gamma, np.float32).reshape(DC, 128).T)
    tri_h = np.where(np.arange(128)[None, :] > np.arange(128)[:, None],
                     np.float32(NEGBIG), np.float32(0.0)).astype(np.float32)
    ident_h = np.eye(128, dtype=np.float32)
    e2_h = np.zeros((2, 128), np.float32)
    e2_h[0, 0:64] = 1.0
    e2_h[1, 64:128] = 1.0
    maps = []
    for b in range(B):
        xb = np.ascontiguousarray(np.asarray(L0_attn_x[b], np.float32).T)
        maps.append({
            "xin": xb, "wfcT": wfcT_h, "wprojT": wprojT_h, "wq": wq_h,
            "wkk": wkk_h, "wvv": wvv_h, "wo": wo_h, "gamd": gam_h,
            "trid": tri_h, "identd": ident_h, "e2d": e2_h,
        })
    return maps


def kernel(L0_attn_x, L0_attn_gamma, L0_attn_Wq, L0_attn_Wkv, L0_attn_Wo,
           L0_ffn_W_fc, L0_ffn_W_proj):
    B, S, _ = L0_attn_x.shape
    maps = make_inputs(L0_attn_x, L0_attn_gamma, L0_attn_Wq, L0_attn_Wkv,
                       L0_attn_Wo, L0_ffn_W_fc, L0_ffn_W_proj)
    nc_prog = _get_program(S, "full")
    res = run_bass_kernel_spmd(nc_prog, maps, core_ids=list(range(B)))
    out = np.stack([np.asarray(r["xout"], np.float32).T for r in res.results],
                   axis=0)
    return np.ascontiguousarray(out)


# version 39
# speedup vs baseline: 1.1564x; 1.1564x over previous
import contextlib
import numpy as np
import concourse.bass as bass
import concourse.mybir as mybir
from concourse.bass_utils import run_bass_kernel_spmd

F32 = mybir.dt.float32
BF16 = mybir.dt.bfloat16
AF = mybir.ActivationFunctionType
ALU = mybir.AluOpType

THETA = (0.24419431388378143, 0.4639692008495331, 0.6861860156059265,
         0.9059608578681946, 1.083978533744812)

D = 1024
DC = 8
FF = 4096
FCN = 32
H = 16
DH = 64
NP = 8
EPS = 1e-5
NEGBIG = -1.0e30

ENGS = ("T", "V", "A", "S", "G")


class Prog:

    def __init__(self):
        self.streams = {e: [] for e in ENGS}
        self.sems = {}

    def inc(self, name, n=1):
        self.sems[name] = self.sems.get(name, 0) + n
        return self.sems[name]

    def val(self, name):
        return self.sems.get(name, 0)

    def emit(self, eng, fn, inc=None):
        if inc is not None:
            self.inc(inc[0], inc[1])

        def run(E, SEM, fn=fn, inc=inc):
            ins = fn(E)
            if inc is not None:
                ins.then_inc(SEM[inc[0]], inc[1])
        self.streams[eng].append(run)

    def wait(self, eng, name, val=None):
        v = self.val(name) if val is None else val
        if v <= 0:
            return
        self.streams[eng].append(
            lambda E, SEM, name=name, v=v: E.wait_ge(SEM[name], v))

    def drain(self, eng):
        self.streams[eng].append(lambda E, SEM: E.drain())


def build_program(S=2048, stage="full"):
    NT = S // 512
    NQB = S // 256
    nc = bass.Bass("TRN2", target_bir_lowering=False, debug=False)

    xin = nc.dram_tensor("xin", [D, S], F32, kind="ExternalInput").ap()
    wfcT = nc.dram_tensor("wfcT", [D, FF], F32, kind="ExternalInput").ap()
    wprojT = nc.dram_tensor("wprojT", [FF, D], F32, kind="ExternalInput").ap()
    wq = nc.dram_tensor("wq", [D, D], F32, kind="ExternalInput").ap()
    wkk = nc.dram_tensor("wkk", [D, 128], F32, kind="ExternalInput").ap()
    wvv = nc.dram_tensor("wvv", [D, DH], F32, kind="ExternalInput").ap()
    wo = nc.dram_tensor("wo", [D, D], F32, kind="ExternalInput").ap()
    gamd = nc.dram_tensor("gamd", [128, 8], F32, kind="ExternalInput").ap()
    trid = nc.dram_tensor("trid", [128, 128], F32, kind="ExternalInput").ap()
    identd = nc.dram_tensor("identd", [128, 128], F32, kind="ExternalInput").ap()
    e2d = nc.dram_tensor("e2d", [2, 128], F32, kind="ExternalInput").ap()
    xout = nc.dram_tensor("xout", [D, S], F32, kind="ExternalOutput").ap()
    mu_d = nc.dram_tensor("mu_d", [1, S], F32, kind="Internal").ap()
    m2_d = nc.dram_tensor("m2_d", [1, S], F32, kind="Internal").ap()
    r_d = nc.dram_tensor("r_d", [1, S], F32, kind="Internal").ap()
    t2_d = nc.dram_tensor("t2_d", [1, S], F32, kind="Internal").ap()
    nc_d = nc.dram_tensor("nc_d", [1, 8], F32, kind="Internal").ap()
    wp16 = nc.dram_tensor("wp16", [8 * 128, 4096], BF16, kind="Internal").ap()
    wf16 = nc.dram_tensor("wf16", [DC * 128, FF], BF16, kind="Internal").ap()

    X = nc.alloc_sbuf_tensor("X", [128, DC * S], F32)
    WB = nc.alloc_sbuf_tensor("WB", [128, 32768], BF16)
    HB = nc.alloc_sbuf_tensor("HB", [128, 16384], BF16)
    Q3 = nc.alloc_sbuf_tensor("Q3", [128, 16384], BF16)
    MU_t = nc.alloc_sbuf_tensor("MU_t", [128, 512], F32)
    R_t = nc.alloc_sbuf_tensor("R_t", [128, 512], F32)
    stg = nc.alloc_sbuf_tensor("stg", [1, 1024], F32)
    FJ = S // 128
    acc = nc.alloc_sbuf_tensor("acc", [128, 512], F32)
    ident = acc[:, 0:128]
    ysb = nc.alloc_sbuf_tensor("ysb", [128, 1024], F32)
    T2_t = ysb[:, 0:512]
    fold = ysb[:, 512:512 + 8 * FJ]
    sacc = nc.alloc_sbuf_tensor("sacc", [128, 128], F32)
    rs16 = nc.alloc_sbuf_tensor("rs16", [128, 16], F32)
    e2 = nc.alloc_sbuf_tensor("e2", [2, 128], BF16)
    ncb = nc.alloc_sbuf_tensor("ncb", [128, 8], F32)
    tri = nc.alloc_sbuf_tensor("tri", [128, 128], BF16)
    gam = nc.alloc_sbuf_tensor("gam", [128, 8], F32)
    ones = nc.alloc_sbuf_tensor("ones", [128, 2], BF16)
    crow = nc.alloc_sbuf_tensor("crow", [1, 64], F32)

    def foldv(i):
        return fold[:, i * FJ:(i + 1) * FJ]

    def xsl(dc, t0, n=512):
        return X[:, dc * S + t0: dc * S + t0 + n]

    def wfc_t(dc, fc):
        return WB[:, dc * FF + fc * 128: dc * FF + fc * 128 + 128]

    def h_t(fc, n=512):
        return HB[:, fc * 512: fc * 512 + n]

    def wpe_t(par, fcl, dc):
        base = par * 4096 + fcl * 1024 + dc * 128
        return Q3[:, base: base + 128]

    def wpe_full(par):
        return Q3[:, par * 4096:(par + 1) * 4096]

    def xn_t(dc):
        return Q3[:, 8192 + dc * 512: 8192 + dc * 512 + 512]

    def xb_t(par):
        return Q3[:, 12288 + par * 512: 12288 + par * 512 + 512]

    def xq_t(par):
        return Q3[:, 13312 + par * 512: 13312 + par * 512 + 512]

    def hh_t(par):
        return Q3[:, 14336 + par * 512: 14336 + par * 512 + 512]

    def qs_t(pc, t0, n=512, p0=0, pn=128):
        return Q3[p0:p0 + pn, pc * S + t0: pc * S + t0 + n]

    def a_t(dc):
        return HB[:, dc * 512: dc * 512 + 512]

    def outT_t(pc, q0, n):
        return HB[:, pc * S + q0: pc * S + q0 + n]

    WQP, WOP, P0P, P1P, KTP, VBP, VTP, WKP, WVP, MSP, R2P = (
        0, 8192, 8192, 16384, 24576, 26624, 27648, 30720, 31744, 27648, 29696)

    def wq_t(dc, oc):
        return WB[:, WQP + dc * 1024 + oc * 128: WQP + dc * 1024 + oc * 128 + 128]

    def wo_t(pc, dc):
        return WB[:, WOP + pc * 1024 + dc * 128: WOP + pc * 1024 + dc * 128 + 128]

    def kt2_t(t0, n=512, p0=0, pn=128):
        return WB[p0:p0 + pn, KTP + t0: KTP + t0 + n]

    def vb_t(jc):
        return WB[:, VBP + jc * 64: VBP + jc * 64 + 64]

    def p2_t(par, pc, hl, n=512):
        base = (P0P if par == 0 else P1P) + (pc * 2 + hl) * n
        return WB[:, base: base + n]

    def wkk_t(dc):
        return WB[:, WKP + dc * 128: WKP + dc * 128 + 128]

    def wvv_t(dc):
        return WB[:, WVP + dc * 64: WVP + dc * 64 + 64]

    def vt_t(t0, n):
        return WB[0:64, VTP + t0: VTP + t0 + n]

    def msc_t(c0, n):
        return WB[:, MSP + c0: MSP + c0 + n]

    def r2_t(c0, n):
        return WB[0:2, R2P + c0: R2P + c0 + n]

    def scr_t(n=512, p0=0, pn=128):
        return WB[p0:p0 + pn, VTP: VTP + n]

    PB = [nc.alloc_psum_tensor(f"pb{i}", [128, 512], F32) for i in range(8)]

    P = Prog()

    for dc in range(DC):
        P.emit("S", lambda E, dc=dc: E.dma_start(
            out=xsl(dc, 0, S), in_=xin[dc * 128:(dc + 1) * 128, :]),
            inc=("g_x", 16))
    for dc in range(DC):
        P.emit("G", lambda E, dc=dc: E.dma_start(
            out=WB[:, dc * FF: dc * FF + FF],
            in_=wfcT[dc * 128:(dc + 1) * 128, :]), inc=("g_wfc", 16))
    P.emit("G", lambda E: E.dma_start(out=gam[:], in_=gamd), inc=("g_misc", 16))
    P.emit("G", lambda E: E.dma_start(out=tri[:], in_=trid), inc=("g_tri", 16))

    P.emit("V", lambda E: E.memset(ones[:, 0:1], 1.0 / 1024.0))
    P.drain("V")
    P.emit("V", lambda E: E.memset(ones[:, 1:2], 1.0), inc=("v_on", 1))
    P.drain("V")

    def ln_stats(tag, with_t2, xwait=None):
        P.wait("V", "g_x", 16 * DC)
        P.wait("A", "g_x", 16 * DC)
        P.wait("T", "v_on", 1)
        for tb in range(NT):
            if xwait is not None:
                P.wait("V", xwait[0], xwait[1](tb))
            for dc in range(DC):
                par = dc % 2
                k = tb * DC + dc
                P.wait("V", "t_sx", max(0, P.val("t_sx") - 1))
                P.emit("V", lambda E, dc=dc, tb=tb, par=par: E.tensor_copy(
                    xb_t(par)[:], xsl(dc, tb * 512)), inc=("v_xb", 1))
                P.wait("A", "t_sx", max(0, P.val("t_sx") - 1))
                if xwait is not None:
                    P.wait("A", xwait[0], xwait[1](tb))
                P.emit("A", lambda E, par=par, dc=dc, tb=tb: E.activation(
                    xq_t(par)[:], xsl(dc, tb * 512), AF.Square), inc=("a_sq", 1))
                P.wait("T", "v_xb")
                P.emit("T", lambda E, par=par, dc=dc: E.matmul(
                    PB[6][0:1, :], ones[:, 0:1], xb_t(par)[:],
                    start=(dc == 0), stop=(dc == DC - 1)))
                P.wait("T", "a_sq")
                P.emit("T", lambda E, par=par, dc=dc: E.matmul(
                    PB[7][0:1, :], ones[:, 0:1], xq_t(par)[:],
                    start=(dc == 0), stop=(dc == DC - 1)),
                    inc=("t_sx", 1))
            P.wait("V", "t_sx")
            P.wait("V", "s_sg")
            P.emit("V", lambda E: E.tensor_copy(stg[0:1, 0:512], PB[6][0:1, :]))
            P.drain("V")
            P.emit("V", lambda E: E.tensor_copy(stg[0:1, 512:1024], PB[7][0:1, :]),
                   inc=("v_sg", 1))
            P.wait("S", "v_sg")
            P.emit("S", lambda E, tb=tb: E.dma_start(
                out=mu_d[0:1, tb * 512:tb * 512 + 512], in_=stg[0:1, 0:512]),
                inc=("s_sg", 16))
            P.emit("S", lambda E, tb=tb: E.dma_start(
                out=m2_d[0:1, tb * 512:tb * 512 + 512], in_=stg[0:1, 512:1024]),
                inc=("s_sg", 16))
            P.wait("T", "v_sg")

        P.wait("S", "s_sg")
        P.emit("S", lambda E: E.dma_start(
            out=foldv(0), in_=mu_d.rearrange("o (p n) -> (o p) n", p=128)),
            inc=("s_fl", 16))
        P.emit("S", lambda E: E.dma_start(
            out=foldv(1), in_=m2_d.rearrange("o (p n) -> (o p) n", p=128)),
            inc=("s_fl", 16))
        P.wait("V", "s_fl")
        P.emit("V", lambda E: E.tensor_mul(foldv(2), foldv(0), foldv(0)))
        P.drain("V")
        P.emit("V", lambda E: E.tensor_sub(foldv(2), foldv(1), foldv(2)))
        P.drain("V")
        P.emit("V", lambda E: E.tensor_scalar_add(foldv(3), foldv(2), EPS),
               inc=("v_u", 1))
        P.wait("A", "v_u")
        P.emit("A", lambda E: E.activation(foldv(4), foldv(3), AF.Sqrt),
               inc=("a_s0", 1))
        P.wait("V", "a_s0")
        P.emit("V", lambda E: E.reciprocal(foldv(5), foldv(4)))
        P.drain("V")
        P.emit("V", lambda E: E.tensor_mul(foldv(6), foldv(5), foldv(5)))
        P.drain("V")
        P.emit("V", lambda E: E.tensor_mul(foldv(6), foldv(6), foldv(3)))
        P.drain("V")
        P.emit("V", lambda E: E.tensor_scalar(foldv(6), foldv(6), -0.5, 1.5,
                                              ALU.mult, ALU.add))
        P.drain("V")
        P.emit("V", lambda E: E.tensor_mul(foldv(6), foldv(6), foldv(5)),
               inc=("v_r", 1))
        P.wait("S", "v_r")
        P.emit("S", lambda E: E.dma_start(
            out=r_d.rearrange("o (p n) -> (o p) n", p=128), in_=foldv(6)),
            inc=("s_r", 16))
        if with_t2:
            P.drain("V")
            P.emit("V", lambda E: E.reciprocal(foldv(2), foldv(3)))
            P.drain("V")
            P.emit("V", lambda E: E.tensor_scalar(foldv(2), foldv(2), -EPS,
                                                  1.0 + EPS, ALU.mult, ALU.add),
                   inc=("v_w", 1))
            P.wait("A", "v_w")
            P.emit("A", lambda E: E.activation(foldv(4), foldv(2), AF.Sqrt),
                   inc=("a_s1", 1))
            P.wait("V", "a_s1")
            P.emit("V", lambda E: E.reciprocal(foldv(5), foldv(4)))
            P.drain("V")
            P.emit("V", lambda E: E.tensor_mul(foldv(7), foldv(5), foldv(5)))
            P.drain("V")
            P.emit("V", lambda E: E.tensor_mul(foldv(7), foldv(7), foldv(2)))
            P.drain("V")
            P.emit("V", lambda E: E.tensor_scalar(foldv(7), foldv(7), -0.5, 1.5,
                                                  ALU.mult, ALU.add))
            P.drain("V")
            P.emit("V", lambda E: E.tensor_mul(foldv(7), foldv(7), foldv(5)))
            P.drain("V")
            P.emit("V", lambda E: E.tensor_scalar_mul(foldv(7), foldv(7), 0.125))
            P.drain("V")
            P.emit("V", lambda E: E.tensor_mul(foldv(7), foldv(7), foldv(6)),
                   inc=("v_t2", 1))
            P.wait("S", "v_t2")
            P.emit("S", lambda E: E.dma_start(
                out=t2_d.rearrange("o (p n) -> (o p) n", p=128), in_=foldv(7)),
                inc=("s_t2", 16))

    def ffn(tag, store_out=False):
        ln_stats(tag, with_t2=False,
                 xwait=("v_wo", lambda tb: (tb + 1) * DC) if tag == "f2" else None)
        tg0 = P.val("t_g")
        tye0 = P.val("t_ye")
        tyd0 = P.val("t_yd")
        vyc0 = P.val("v_yc")
        vsp0 = P.val("v_sp")

        def emit_spike(tb, dc):
            if dc == 0:
                P.wait("V", "t_ye", tye0 + (tb + 1) * DC)
            ysrc = (lambda: ysb[:, dc * 512:(dc + 1) * 512]) if dc < 2 \
                else (lambda: PB[dc][:])
            P.emit("V", lambda E, dc=dc: E.tensor_single_scalar(
                acc[:], ysrc(), THETA[0], ALU.is_ge))
            P.drain("V")
            for i in range(1, 5):
                P.emit("V", lambda E, dc=dc, i=i: E.scalar_tensor_tensor(
                    acc[:], ysrc(), THETA[i], acc[:], ALU.is_ge, ALU.add))
                P.drain("V")
            P.emit("V", lambda E, dc=dc, tb=tb: E.scalar_tensor_tensor(
                xsl(dc, tb * 512), acc[:], 0.1, xsl(dc, tb * 512),
                ALU.mult, ALU.add), inc=("v_sp", 1))
            P.drain("V")
            if store_out:
                P.wait("S", "v_sp")
                P.emit("S", lambda E, dc=dc, tb=tb: E.dma_start(
                    out=xout[dc * 128:(dc + 1) * 128, tb * 512:(tb + 1) * 512],
                    in_=xsl(dc, tb * 512)), inc=("s_out", 16))

        def emit_apply(tb):
            P.wait("S", "v_ap")
            P.emit("S", lambda E, tb=tb: E.dma_start(
                out=MU_t[:], in_=bass.AP(mu_d.tensor, tb * 512, [[0, 128], [1, 512]])),
                inc=("s_mu", 16))
            P.wait("S", "s_r")
            P.emit("S", lambda E, tb=tb: E.dma_start(
                out=R_t[:], in_=bass.AP(r_d.tensor, tb * 512, [[0, 128], [1, 512]])),
                inc=("s_mu", 16))
            P.wait("V", "s_mu")
            P.wait("V", "t_g", tg0 + tb * FCN)
            for dc in range(DC):
                P.emit("V", lambda E, dc=dc, tb=tb: E.tensor_sub(
                    acc[:], xsl(dc, tb * 512), MU_t[:]))
                P.drain("V")
                P.emit("V", lambda E, dc=dc: E.tensor_mul(
                    xn_t(dc)[:], acc[:], R_t[:]), inc=("v_ap", 1))
                P.drain("V")

        emit_apply(0)
        for tb in range(NT):
            if tag == "f1" and tb == 1:
                P.wait("A", "g_wfc", 16 * DC)
                for dc in range(DC):
                    P.emit("A", lambda E, dc=dc: E.dma_start(
                        out=wf16[dc * 128:(dc + 1) * 128, :],
                        in_=WB[:, dc * FF: dc * FF + FF]), inc=("s_wf16", 16))
            vap_tb = P.val("v_ap") - DC
            P.wait("T", "v_yc", vyc0 + tb)
            if tag == "f1":
                P.wait("T", "g_wfc", 16 * DC)
            else:
                P.wait("T", "g_wfc2", 16 * DC)
            for fc in range(FCN):
                par = fc % 2
                P.wait("T", "a_rl", max(0, P.val("a_rl") - 1))
                for dc in range(DC):
                    if fc == 0:
                        P.wait("T", "v_ap", vap_tb + dc + 1)
                    P.emit("T", lambda E, par=par, dc=dc, fc=fc: E.matmul(
                        PB[par][:], wfc_t(dc, fc), xn_t(dc)[:],
                        start=(dc == 0), stop=(dc == DC - 1)),
                        inc=("t_g", 1) if dc == DC - 1 else None)
                P.wait("A", "t_g")
                P.wait("A", "v_h", max(0, P.val("v_h") - 1))
                P.emit("A", lambda E, par=par: E.activation(
                    hh_t(par)[:], PB[par][:], AF.Relu), inc=("a_rl", 1))
                P.wait("V", "a_rl")
                P.emit("V", lambda E, par=par, fc=fc: E.tensor_mul(
                    h_t(fc), hh_t(par)[:], hh_t(par)[:]), inc=("v_h", 1))
                if tb > 0 and fc % 4 == 3:
                    emit_spike(tb - 1, fc // 4)
            for e in range(8):
                par = e % 2
                ge = tb * 8 + e
                P.wait("G", "t_ye", max(tye0, tye0 + ge - 1))
                if tag == "f1" and tb == 0 and e >= 2:
                    P.wait("G", f"s_wp16p{e % 2}", 16 * (e // 2))
                if tag == "f2" and tb == 0 and e < 2:
                    P.wait("G", "t_s")
                if tag == "f1" and tb == 0:
                    for fl in range(4):
                        P.emit("G", lambda E, e=e, par=par, fl=fl: E.dma_start(
                            out=Q3[:, par * 4096 + fl * 1024: par * 4096 + fl * 1024 + 1024],
                            in_=wprojT[e * 512 + fl * 128: e * 512 + fl * 128 + 128, :]),
                            inc=(f"g_wp{par}", 16))
                    P.wait("S", f"g_wp{par}")
                    P.emit("S", lambda E, e=e, par=par: E.dma_start(
                        out=wp16[e * 128:(e + 1) * 128, :], in_=wpe_full(par)),
                        inc=(f"s_wp16p{par}", 16))
                else:
                    if tag == "f1" and tb == 1 and e < 2:
                        P.wait("G", f"s_wp16p{e % 2}", 64)
                    P.emit("G", lambda E, e=e, par=par: E.dma_start(
                        out=wpe_full(par), in_=wp16[e * 128:(e + 1) * 128, :]),
                        inc=(f"g_wp{par}", 16))
                P.wait("T", f"g_wp{par}")
                if e == 0:
                    P.wait("T", "v_h")
                for dc in range(DC):
                    if e == 0 and tb > 0:
                        P.wait("T", "v_sp", vsp0 + (tb - 1) * DC + dc + 1)
                    for fcl in range(4):
                        fc = e * 4 + fcl
                        st = (e == 0 and fcl == 0)
                        sp = (e == 7 and fcl == 3)
                        if dc == DC - 1 and fcl == 3:
                            mark = ("t_ye", 1)
                        elif e == 7 and fcl == 3 and dc < 2:
                            mark = ("t_yd", 1)
                        else:
                            mark = None
                        P.emit("T", lambda E, dc=dc, par=par, fcl=fcl, fc=fc,
                               st=st, sp=sp: E.matmul(
                            PB[dc][:], wpe_t(par, fcl, dc), h_t(fc),
                            start=st, stop=sp), inc=mark)
            if tb + 1 < NT:
                emit_apply(tb + 1)
            P.wait("V", "t_yd", tyd0 + (tb + 1) * 2)
            P.emit("V", lambda E: E.tensor_copy(ysb[:, 0:512], PB[0][:]))
            P.drain("V")
            P.emit("V", lambda E: E.tensor_copy(ysb[:, 512:1024], PB[1][:]),
                   inc=("v_yc", 1))
            P.drain("V")
            if tb == NT - 1:
                for dc in range(DC):
                    emit_spike(tb, dc)

    def attention():
        ln_stats("a", with_t2=True)
        P.wait("G", "t_g")
        P.wait("G", "t_ye")
        P.wait("G", "s_wf16", 16 * DC)
        for dc in range(DC):
            P.emit("G", lambda E, dc=dc: E.dma_start(
                out=WB[:, WQP + dc * 1024: WQP + dc * 1024 + 1024],
                in_=wq[dc * 128:(dc + 1) * 128, :]), inc=("g_wq", 16))
        for dc in range(DC):
            P.emit("G", lambda E, dc=dc: E.dma_start(
                out=WB[:, WKP + dc * 128: WKP + dc * 128 + 128],
                in_=wkk[dc * 128:(dc + 1) * 128, :]), inc=("g_wk", 16))
            P.emit("G", lambda E, dc=dc: E.dma_start(
                out=WB[:, WVP + dc * 64: WVP + dc * 64 + 64],
                in_=wvv[dc * 128:(dc + 1) * 128, :]), inc=("g_wk", 16))
        P.wait("V", "g_wq", 16 * DC)
        P.wait("V", "g_misc", 16)
        for dc in range(DC):
            P.emit("V", lambda E, dc=dc: E.tensor_scalar_mul(
                WB[:, WQP + dc * 1024: WQP + dc * 1024 + 1024],
                WB[:, WQP + dc * 1024: WQP + dc * 1024 + 1024],
                gam[:, dc:dc + 1]), inc=("v_gw", 1))
            P.drain("V")

        P.wait("V", "s_wp16p0", P.val("s_wp16p0"))
        P.wait("V", "s_wp16p1", P.val("s_wp16p1"))
        vapa0 = P.val("v_apa")
        for tb in range(NT):
            P.wait("S", "v_apa")
            P.wait("S", "v_qs")
            P.emit("S", lambda E, tb=tb: E.dma_start(
                out=MU_t[:], in_=bass.AP(mu_d.tensor, tb * 512, [[0, 128], [1, 512]])),
                inc=("s_muo", 16))
            P.wait("S", "s_r")
            P.emit("S", lambda E, tb=tb: E.dma_start(
                out=R_t[:], in_=bass.AP(r_d.tensor, tb * 512, [[0, 128], [1, 512]])),
                inc=("s_mua", 16))
            P.wait("S", "s_t2")
            P.emit("S", lambda E, tb=tb: E.dma_start(
                out=T2_t, in_=bass.AP(t2_d.tensor, tb * 512, [[0, 128], [1, 512]])),
                inc=("s_mua", 16))
            P.wait("V", "s_muo", 16 * (tb + 1))
            P.wait("V", "v_gw", DC)
            P.wait("V", "t_kv", tb)
            P.wait("V", "t_vt", tb)
            P.wait("V", "t_qs", tb * DC)
            for dc in range(DC):
                P.emit("V", lambda E, dc=dc, tb=tb: E.tensor_sub(
                    a_t(dc)[:], xsl(dc, tb * 512), MU_t[:]), inc=("v_apa", 1))
                P.drain("V")
            P.wait("T", "g_wk", 16 * 2 * DC)
            P.wait("T", "v_kv", tb)
            for dc in range(DC):
                P.wait("T", "v_apa", vapa0 + tb * DC + dc + 1)
                P.emit("T", lambda E, dc=dc: E.matmul(
                    PB[3][:], wkk_t(dc), a_t(dc)[:],
                    start=(dc == 0), stop=(dc == DC - 1)),
                    inc=("t_kv", 1) if dc == DC - 1 else None)
            P.wait("T", "v_vt", tb)
            for dc in range(DC):
                P.emit("T", lambda E, dc=dc: E.matmul(
                    PB[4][0:64, :], wvv_t(dc), a_t(dc)[:],
                    start=(dc == 0), stop=(dc == DC - 1)),
                    inc=("t_vt", 1) if dc == DC - 1 else None)
            P.wait("V", "s_mua")
            P.wait("V", "t_kv")
            P.emit("V", lambda E, tb=tb: E.tensor_mul(
                kt2_t(tb * 512), PB[3][:], R_t[:]), inc=("v_kv", 1))
            P.drain("V")
            P.wait("V", "t_vt")
            P.emit("V", lambda E, tb=tb: E.tensor_mul(
                vt_t(tb * 512, 512), PB[4][0:64, :], R_t[0:64, :]),
                inc=("v_vt", 1))
            P.drain("V")
            for oc in range(DC):
                bank = 5 + (oc % 2)
                P.wait("T", "v_qs", max(0, P.val("v_qs") - 1))
                for dc in range(DC):
                    P.emit("T", lambda E, oc=oc, dc=dc, bank=bank: E.matmul(
                        PB[bank][:], wq_t(dc, oc), a_t(dc)[:],
                        start=(dc == 0), stop=(dc == DC - 1)),
                        inc=("t_qs", 1) if dc == DC - 1 else None)
                P.wait("V", "t_qs")
                P.emit("V", lambda E, oc=oc, tb=tb, bank=bank: E.tensor_mul(
                    qs_t(oc, tb * 512), PB[bank][:], T2_t), inc=("v_qs", 1))
                P.drain("V")
                P.emit("V", lambda E, oc=oc, tb=tb: E.tensor_mul(
                    WB[:, 8192:8704], qs_t(oc, tb * 512), qs_t(oc, tb * 512)),
                    inc=("v_qn2", 1))
                P.drain("V")
                P.wait("T", "v_qn2")
                P.wait("T", "v_qn")
                P.emit("T", lambda E: E.matmul(
                    PB[7][0:1, :], ones[:, 1:2], WB[:, 8192:8704],
                    start=True, stop=True), inc=("t_qn", 1))
                P.wait("V", "t_qn")
                P.emit("V", lambda E, oc=oc, tb=tb: E.reduce_max(
                    crow[0:1, oc * 4 + tb: oc * 4 + tb + 1], PB[7][0:1, :],
                    axis=mybir.AxisListType.X), inc=("v_qn", 1))
                P.drain("V")

        P.wait("S", "v_apa")
        P.emit("S", lambda E: E.dma_start(out=ident, in_=identd), inc=("s_tri", 16))
        P.wait("S", "v_vt")
        P.emit("S", lambda E: E.dma_start_transpose(
            WB[:, VBP:VBP + S // 2].rearrange("p (c n) -> p c n", n=64),
            WB[0:64, VTP:VTP + S].rearrange("p (c n) -> p c n", n=128)),
            inc=("s_vb", 16))

        P.wait("A", "t_qn")
        P.wait("A", "s_vb", 16)
        P.wait("T", "v_qn")
        for tt in range(NT):
            P.wait("A", "v_kv")
            P.wait("A", "t_kn")
            P.emit("A", lambda E, tt=tt: E.activation(
                scr_t(512, 0, 64), kt2_t(tt * 512, 512, 0, 64), AF.Square),
                inc=("a_kn", 1))
            P.wait("T", "a_kn")
            P.wait("T", "v_kn")
            P.emit("T", lambda E: E.matmul(
                PB[7][0:1, :], ones[0:64, 1:2], scr_t(512, 0, 64),
                start=True, stop=True), inc=("t_kn", 1))
            P.wait("V", "t_kn")
            P.emit("V", lambda E, tt=tt: E.reduce_max(
                crow[0:1, 32 + tt: 33 + tt], PB[7][0:1, :],
                axis=mybir.AxisListType.X), inc=("v_kn", 1))
        P.wait("V", "v_qn")
        P.wait("V", "v_kn")
        for pc in range(NP):
            P.emit("V", lambda E, pc=pc: E.reduce_max(
                crow[0:1, 40 + pc: 41 + pc], crow[0:1, pc * 4: pc * 4 + NT],
                axis=mybir.AxisListType.X))
            P.drain("V")
        P.emit("V", lambda E: E.reduce_max(
            crow[0:1, 48:49], crow[0:1, 32: 32 + NT], axis=mybir.AxisListType.X))
        P.drain("V")
        P.emit("V", lambda E: E.tensor_single_scalar(
            crow[0:1, 40:48], crow[0:1, 40:48], crow[0:1, 48:49], ALU.mult),
            inc=("v_c1", 1))
        P.wait("A", "v_c1")
        P.emit("A", lambda E: E.activation(
            crow[0:1, 56:64], crow[0:1, 40:48], AF.Sqrt), inc=("a_c", 1))
        P.wait("V", "a_c")
        P.emit("V", lambda E: E.tensor_scalar_mul(
            crow[0:1, 56:64], crow[0:1, 56:64], -1.0), inc=("v_c2", 1))
        P.wait("S", "v_c2")
        P.emit("S", lambda E: E.dma_start(out=nc_d, in_=crow[0:1, 56:64]),
               inc=("s_c", 16))
        P.wait("S", "s_c")
        P.emit("S", lambda E: E.dma_start(
            out=ncb[:], in_=bass.AP(nc_d.tensor, 0, [[0, 128], [1, 8]])),
            inc=("s_c", 16))

        P.wait("T", "s_vb", 16)
        P.wait("T", "v_kv")
        P.wait("T", "v_vt")
        P.wait("T", "v_qs")
        P.wait("A", "s_c", 32)
        P.wait("A", "s_tri", 16)
        P.wait("T", "s_tri", 16)
        P.wait("V", "g_tri", 16)
        P.wait("S", "v_gw", DC)
        P.emit("G", lambda E: E.dma_start(out=e2[:], in_=e2d), inc=("g_e2", 16))
        P.wait("T", "g_e2", 16)

        NQT = S // 128
        tr_done = {}
        ex_done = {}
        unit_of = {}
        uidx = 0
        pend_av = None
        pend_msc = None

        def emit_av(qt, jb, u):
            ext = (qt + 1) * 128
            jbn = (ext + 511) // 512
            N = min(512, ext - jb * 512)
            jcn = N // 128
            ob = 4 if qt % 2 == 0 else 6
            P.wait("T", "s_tr", tr_done[u])
            P.wait("T", "t_av", u)
            if jb == 0:
                P.wait("T", "v_msc", max(0, 2 * (qt - 1)))
            for jcl in range(jcn):
                jc = jb * 4 + jcl
                st = (jb == 0 and jcl == 0)
                sp = (jb == jbn - 1 and jcl == jcn - 1)
                for hl in range(2):
                    for half in range(2):
                        lastmm = (jcl == jcn - 1 and hl == 1 and half == 1)
                        off = ((half * 8 + hl) * jcn + jcl) * 128
                        rhs = bass.AP(WB, off, [[WB.shape[1], 128],
                                                [2 * jcn * 128, 4], [1, 128]])
                        P.emit("T", lambda E, jc=jc, rhs=rhs, hl=hl, half=half,
                               st=st, sp=sp, ob=ob: E.matmul(
                                   PB[ob + half][hl * 64:(hl + 1) * 64, 0:512],
                                   vb_t(jc), rhs,
                                   start=st, stop=sp, skip_group_check=True,
                                   tile_position=(0, hl * 64)),
                               inc=("t_av", 1) if lastmm else None)

        def emit_msc(qt):
            ob = 4 if qt % 2 == 0 else 6
            spb = 2 * (1 - (P.val("t_s") // 2) % 2)
            P.wait("V", "a_ex", ex_done[qt])
            po = (qt % 2) * 64
            P.emit("V", lambda E, po=po: E.tensor_reduce(
                rs16[:], sacc[:, po:po + 64].rearrange("p (g j) -> p g j", j=4),
                axis=mybir.AxisListType.X, op=ALU.add))
            P.drain("V")
            P.emit("V", lambda E: E.reciprocal(rs16[:], rs16[:]),
                   inc=("v_rs", 1))
            P.drain("V")
            P.wait("T", "v_rs", qt + 1)
            P.wait("T", "a_ex")
            for pc in range(NP):
                P.emit("T", lambda E, pc=pc, mb=spb: E.transpose(
                    PB[mb + pc // 4][0:2, (pc % 4) * 128:(pc % 4) * 128 + 128],
                    rs16[:, pc * 2: pc * 2 + 2], ident),
                    inc=("t_mst", 1) if pc == NP - 1 else None)
            P.wait("V", "t_mst", qt + 1)
            P.emit("V", lambda E, mb=spb: E.tensor_copy(
                r2_t(0, 512), PB[mb][0:2, 0:512]))
            P.drain("V")
            P.emit("V", lambda E, mb=spb: E.tensor_copy(
                r2_t(512, 512), PB[mb + 1][0:2, 0:512]), inc=("v_r2", 1))
            P.wait("T", "v_r2", qt + 1)
            for half in range(2):
                P.emit("T", lambda E, half=half, mb=spb: E.matmul(
                    PB[mb + half][:, 0:512], e2[:], r2_t(half * 512, 512),
                    start=True, stop=True), inc=("t_msm", 1) if half else None)
            P.wait("V", "t_msm", qt + 1)
            for half in range(2):
                P.emit("V", lambda E, half=half, mb=spb: E.tensor_copy(
                    msc_t(half * 512, 512), PB[mb + half][:, 0:512]))
                P.drain("V")
            P.emit("V", lambda E: E.engine_nop(), inc=("v_mc", 1))
            P.wait("V", "t_av", unit_of[(qt, ((qt + 1) * 128 + 511) // 512 - 1)] + 1)
            for half in range(2):
                dstbase = half * 4 * S + qt * 128
                dst3 = bass.AP(HB, dstbase, [[HB.shape[1], 128], [S, 4], [1, 128]])
                P.emit("V", lambda E, half=half, ob=ob, dst3=dst3: E.tensor_tensor(
                    dst3,
                    PB[ob + half][:, 0:512].rearrange("p (pc n) -> p pc n", n=128),
                    msc_t(half * 512, 512).rearrange("p (pc n) -> p pc n", n=128),
                    op=ALU.mult), inc=("v_msc", 1))
                P.drain("V")

        for qt in range(NQT):
            ext = (qt + 1) * 128
            jbn = (ext + 511) // 512
            par = qt % 2
            po = par * 64
            if pend_msc is not None:
                emit_msc(pend_msc)
                pend_msc = None
            P.wait("V", "a_ex")
            P.emit("V", lambda E, po=po: E.memset(sacc[:, po:po + 64], 0.0),
                   inc=("v_sz", 1))
            for jb in range(jbn):
                u = uidx
                uidx += 1
                unit_of[(qt, jb)] = u
                upar = u % 2
                N = min(512, ext - jb * 512)
                for pc in range(NP):
                    i = P.val("t_s") // 2
                    bA = (i % 2) * 2
                    bB = bA + 1
                    P.wait("T", "a_ex", max(0, P.val("a_ex") - 2))
                    P.wait("T", "v_mc")
                    for rep in range(2):
                        P.emit("T", lambda E, pc=pc, qt=qt, jb=jb, N=N, bA=bA:
                               E.matmul(PB[bA][:, 0:N],
                                        qs_t(pc, qt * 128, 128, 0, 64),
                                        kt2_t(jb * 512, N, 0, 64),
                                        start=True, stop=True,
                                        skip_group_check=True))
                        P.emit("T", lambda E, pc=pc, qt=qt, jb=jb, N=N, bB=bB:
                               E.matmul(PB[bB][:, 0:N],
                                        qs_t(pc, qt * 128, 128, 64, 64),
                                        kt2_t(jb * 512, N, 64, 64),
                                        start=True, stop=True,
                                        skip_group_check=True),
                               inc=("t_s", 2) if rep == 1 else None)
                    diag = qt * 128 - jb * 512
                    if 0 <= diag < N:
                        P.wait("V", "t_s")
                        for b in (bA, bB):
                            P.emit("V", lambda E, b=b, diag=diag: E.tensor_add(
                                PB[b][:, diag:diag + 128],
                                PB[b][:, diag:diag + 128], tri[:]),
                                inc=("v_msk", 1))
                            P.drain("V")
                    if 0 <= diag < N:
                        P.wait("A", "v_msk")
                    P.wait("A", "t_s")
                    if jb == 0 and pc == 0:
                        P.wait("A", "v_sz")
                    if u >= 2 and pc == 0:
                        P.wait("A", "s_tr", tr_done[u - 2])
                    for hl in range(2):
                        g = pc * 2 + hl
                        P.emit("A", lambda E, hl=hl, par=upar, jb=jb, N=N,
                               b=(bA, bB)[hl], pc=pc, g=g, po=po: E.activation(
                                   p2_t(par, pc, hl, N), PB[b][:, 0:N],
                                   AF.Exp, bias=ncb[:, pc:pc + 1], scale=1.0,
                                   accum_out=sacc[:, po + g * 4 + jb:
                                                  po + g * 4 + jb + 1]),
                               inc=("a_ex", 1))
                P.wait("S", "a_ex")
                P.wait("S", "t_av", u)
                jcn = N // 128
                srcbase = P0P if upar == 0 else P1P
                src2 = bass.AP(WB, srcbase, [[WB.shape[1], 128], [1, 16 * N]])
                dst3 = bass.AP(WB, 0,
                               [[WB.shape[1], 128], [128, 16 * jcn], [1, 128]])
                P.emit("S", lambda E, dst3=dst3, src2=src2:
                       E.dma_start_transpose(dst3, src2), inc=("s_tr", 16))
                tr_done[u] = P.val("s_tr")
                if pend_msc is not None:
                    emit_msc(pend_msc)
                    pend_msc = None
                if pend_av is not None:
                    pqt, pjb, pu = pend_av
                    emit_av(pqt, pjb, pu)
                    pext = (pqt + 1) * 128
                    if pjb == (pext + 511) // 512 - 1:
                        pend_msc = pqt
                pend_av = (qt, jb, u)
            ex_done[qt] = P.val("a_ex")
        if pend_msc is not None:
            emit_msc(pend_msc)
            pend_msc = None
        pqt, pjb, pu = pend_av
        emit_av(pqt, pjb, pu)
        emit_msc(pqt)

        P.wait("G", "a_ex")
        P.wait("G", "s_tr")
        for pc in range(NP):
            P.emit("G", lambda E, pc=pc: E.dma_start(
                out=WB[:, WOP + pc * 1024: WOP + pc * 1024 + 1024],
                in_=wo[pc * 128:(pc + 1) * 128, :]), inc=("g_wo", 16))
        if stage == "attn":
            otd = nc.dram_tensor("otd", [128, 16384], BF16, kind="ExternalOutput").ap()
            P.wait("S", "v_msc")
            P.emit("S", lambda E: E.dma_start(out=otd, in_=HB[:]), inc=("s_out", 16))
        P.wait("T", "v_msc")
        P.wait("T", "g_wo", 16 * NP)
        for tt in range(NT):
            for dc in range(DC):
                bank = (tt * DC + dc) % 2
                P.wait("T", "v_wo", max(0, P.val("v_wo") - 1))
                for pc in range(NP):
                    P.emit("T", lambda E, pc=pc, dc=dc, tt=tt, bank=bank: E.matmul(
                        PB[bank][:], wo_t(pc, dc), outT_t(pc, tt * 512, 512),
                        start=(pc == 0), stop=(pc == NP - 1)),
                        inc=("t_wo", 1) if pc == NP - 1 else None)
                P.wait("V", "t_wo")
                P.emit("V", lambda E, dc=dc, tt=tt, bank=bank: E.tensor_add(
                    xsl(dc, tt * 512), PB[bank][:], xsl(dc, tt * 512)),
                    inc=("v_wo", 1))
                P.drain("V")

        P.wait("G", "t_av")
        for dc in (0, 1, 4, 5):
            P.emit("G", lambda E, dc=dc: E.dma_start(
                out=WB[:, dc * FF: dc * FF + FF], in_=wf16[dc * 128:(dc + 1) * 128, :]),
                inc=("g_wfc2", 16))
        P.wait("G", "v_msc")
        P.wait("G", "t_msm")
        for dc in (6, 7):
            P.emit("G", lambda E, dc=dc: E.dma_start(
                out=WB[:, dc * FF: dc * FF + FF], in_=wf16[dc * 128:(dc + 1) * 128, :]),
                inc=("g_wfc2", 16))
        P.wait("G", "t_wo")
        for dc in (2, 3):
            P.emit("G", lambda E, dc=dc: E.dma_start(
                out=WB[:, dc * FF: dc * FF + FF], in_=wf16[dc * 128:(dc + 1) * 128, :]),
                inc=("g_wfc2", 16))


    ffn("f1", store_out=(stage == "ffn1"))
    if stage != "ffn1":
        attention()
        if stage == "attn":
            P.wait("S", "v_wo")
            for dc in range(DC):
                P.emit("S", lambda E, dc=dc: E.dma_start(
                    out=xout[dc * 128:(dc + 1) * 128, :], in_=xsl(dc, 0, S)),
                    inc=("s_out", 16))
        else:
            for e in ("V", "A", "T", "G"):
                P.wait(e, "t_s")
                P.wait(e, "t_av")
                P.wait(e, "a_ex")
                P.wait(e, "t_mst")
            ffn("f2", store_out=True)
    P.wait("S", "s_out")
    for name in list(P.sems):
        if name.startswith("s_"):
            P.wait("S", name)
        elif name.startswith("g_"):
            P.wait("G", name)

    semnames = sorted(P.sems.keys())
    with contextlib.ExitStack() as stack:
        sems = {n: stack.enter_context(nc.semaphore(n)) for n in semnames}
        with nc.Block() as block:
            @block.tensor
            def _(E):
                for fn in P.streams["T"]:
                    fn(E, sems)

            @block.vector
            def _(E):
                for fn in P.streams["V"]:
                    fn(E, sems)

            @block.scalar
            def _(E):
                for fn in P.streams["A"]:
                    fn(E, sems)

            @block.sync
            def _(E):
                for fn in P.streams["S"]:
                    fn(E, sems)

            @block.gpsimd
            def _(E):
                for fn in P.streams["G"]:
                    fn(E, sems)
    return nc


_cache = {}


def _get_program(S=2048, stage="full"):
    key = (S, stage)
    if key not in _cache:
        _cache[key] = build_program(S, stage)
    return _cache[key]


def make_inputs(L0_attn_x, L0_attn_gamma, L0_attn_Wq, L0_attn_Wkv, L0_attn_Wo,
                L0_ffn_W_fc, L0_ffn_W_proj):
    B = L0_attn_x.shape[0]
    wfcT_h = np.ascontiguousarray(np.asarray(L0_ffn_W_fc, np.float32).T)
    wprojT_h = np.ascontiguousarray(np.asarray(L0_ffn_W_proj, np.float32).T)
    wq_h = np.ascontiguousarray(np.asarray(L0_attn_Wq, np.float32))
    wkv = np.asarray(L0_attn_Wkv, np.float32)
    wkk_h = np.ascontiguousarray(np.concatenate([wkv[:, 0:DH], wkv[:, 0:DH]], axis=1))
    wvv_h = np.ascontiguousarray(wkv[:, DH:2 * DH])
    wo_h = np.ascontiguousarray(np.asarray(L0_attn_Wo, np.float32))
    gam_h = np.ascontiguousarray(
        np.asarray(L0_attn_gamma, np.float32).reshape(DC, 128).T)
    tri_h = np.where(np.arange(128)[None, :] > np.arange(128)[:, None],
                     np.float32(NEGBIG), np.float32(0.0)).astype(np.float32)
    ident_h = np.eye(128, dtype=np.float32)
    e2_h = np.zeros((2, 128), np.float32)
    e2_h[0, 0:64] = 1.0
    e2_h[1, 64:128] = 1.0
    maps = []
    for b in range(B):
        xb = np.ascontiguousarray(np.asarray(L0_attn_x[b], np.float32).T)
        maps.append({
            "xin": xb, "wfcT": wfcT_h, "wprojT": wprojT_h, "wq": wq_h,
            "wkk": wkk_h, "wvv": wvv_h, "wo": wo_h, "gamd": gam_h,
            "trid": tri_h, "identd": ident_h, "e2d": e2_h,
        })
    return maps


def kernel(L0_attn_x, L0_attn_gamma, L0_attn_Wq, L0_attn_Wkv, L0_attn_Wo,
           L0_ffn_W_fc, L0_ffn_W_proj):
    B, S, _ = L0_attn_x.shape
    maps = make_inputs(L0_attn_x, L0_attn_gamma, L0_attn_Wq, L0_attn_Wkv,
                       L0_attn_Wo, L0_ffn_W_fc, L0_ffn_W_proj)
    nc_prog = _get_program(S, "full")
    res = run_bass_kernel_spmd(nc_prog, maps, core_ids=list(range(B)))
    out = np.stack([np.asarray(r["xout"], np.float32).T for r in res.results],
                   axis=0)
    return np.ascontiguousarray(out)


# version 40
# speedup vs baseline: 1.1862x; 1.0258x over previous
import contextlib
import numpy as np
import concourse.bass as bass
import concourse.mybir as mybir
from concourse.bass_utils import run_bass_kernel_spmd

F32 = mybir.dt.float32
BF16 = mybir.dt.bfloat16
AF = mybir.ActivationFunctionType
ALU = mybir.AluOpType

THETA = (0.24419431388378143, 0.4639692008495331, 0.6861860156059265,
         0.9059608578681946, 1.083978533744812)

D = 1024
DC = 8
FF = 4096
FCN = 32
H = 16
DH = 64
NP = 8
EPS = 1e-5
NEGBIG = -1.0e30

ENGS = ("T", "V", "A", "S", "G")


class Prog:

    def __init__(self):
        self.streams = {e: [] for e in ENGS}
        self.sems = {}

    def inc(self, name, n=1):
        self.sems[name] = self.sems.get(name, 0) + n
        return self.sems[name]

    def val(self, name):
        return self.sems.get(name, 0)

    def emit(self, eng, fn, inc=None):
        if inc is not None:
            self.inc(inc[0], inc[1])

        def run(E, SEM, fn=fn, inc=inc):
            ins = fn(E)
            if inc is not None:
                ins.then_inc(SEM[inc[0]], inc[1])
        self.streams[eng].append(run)

    def wait(self, eng, name, val=None):
        v = self.val(name) if val is None else val
        if v <= 0:
            return
        self.streams[eng].append(
            lambda E, SEM, name=name, v=v: E.wait_ge(SEM[name], v))

    def drain(self, eng):
        self.streams[eng].append(lambda E, SEM: E.drain())


def build_program(S=2048, stage="full"):
    NT = S // 512
    NQB = S // 256
    nc = bass.Bass("TRN2", target_bir_lowering=False, debug=False)

    xin = nc.dram_tensor("xin", [D, S], F32, kind="ExternalInput").ap()
    wfcT = nc.dram_tensor("wfcT", [D, FF], F32, kind="ExternalInput").ap()
    wprojT = nc.dram_tensor("wprojT", [FF, D], F32, kind="ExternalInput").ap()
    wq = nc.dram_tensor("wq", [D, D], F32, kind="ExternalInput").ap()
    wkk = nc.dram_tensor("wkk", [D, 128], F32, kind="ExternalInput").ap()
    wvv = nc.dram_tensor("wvv", [D, DH], F32, kind="ExternalInput").ap()
    wo = nc.dram_tensor("wo", [D, D], F32, kind="ExternalInput").ap()
    gamd = nc.dram_tensor("gamd", [128, 8], F32, kind="ExternalInput").ap()
    trid = nc.dram_tensor("trid", [128, 128], F32, kind="ExternalInput").ap()
    identd = nc.dram_tensor("identd", [128, 128], F32, kind="ExternalInput").ap()
    e2d = nc.dram_tensor("e2d", [2, 128], F32, kind="ExternalInput").ap()
    xout = nc.dram_tensor("xout", [D, S], F32, kind="ExternalOutput").ap()
    mu_d = nc.dram_tensor("mu_d", [1, S], F32, kind="Internal").ap()
    m2_d = nc.dram_tensor("m2_d", [1, S], F32, kind="Internal").ap()
    r_d = nc.dram_tensor("r_d", [1, S], F32, kind="Internal").ap()
    t2_d = nc.dram_tensor("t2_d", [1, S], F32, kind="Internal").ap()
    nc_d = nc.dram_tensor("nc_d", [1, 8], F32, kind="Internal").ap()
    wp16 = nc.dram_tensor("wp16", [8 * 128, 4096], BF16, kind="Internal").ap()
    wf16 = nc.dram_tensor("wf16", [DC * 128, FF], BF16, kind="Internal").ap()

    X = nc.alloc_sbuf_tensor("X", [128, DC * S], F32)
    WB = nc.alloc_sbuf_tensor("WB", [128, 32768], BF16)
    HB = nc.alloc_sbuf_tensor("HB", [128, 16384], BF16)
    Q3 = nc.alloc_sbuf_tensor("Q3", [128, 16384], BF16)
    MU_t = nc.alloc_sbuf_tensor("MU_t", [128, 512], F32)
    R_t = nc.alloc_sbuf_tensor("R_t", [128, 512], F32)
    stg = nc.alloc_sbuf_tensor("stg", [1, 1024], F32)
    FJ = S // 128
    acc = nc.alloc_sbuf_tensor("acc", [128, 512], F32)
    ident = acc[:, 0:128]
    ysb = nc.alloc_sbuf_tensor("ysb", [128, 1024], F32)
    T2_t = ysb[:, 0:512]
    fold = ysb[:, 512:512 + 8 * FJ]
    sacc = nc.alloc_sbuf_tensor("sacc", [128, 128], F32)
    rs16 = nc.alloc_sbuf_tensor("rs16", [128, 16], F32)
    e2 = nc.alloc_sbuf_tensor("e2", [2, 128], BF16)
    ncb = nc.alloc_sbuf_tensor("ncb", [128, 8], F32)
    tri = nc.alloc_sbuf_tensor("tri", [128, 128], BF16)
    gam = nc.alloc_sbuf_tensor("gam", [128, 8], F32)
    ones = nc.alloc_sbuf_tensor("ones", [128, 2], BF16)
    crow = nc.alloc_sbuf_tensor("crow", [1, 64], F32)

    def foldv(i):
        return fold[:, i * FJ:(i + 1) * FJ]

    def xsl(dc, t0, n=512):
        return X[:, dc * S + t0: dc * S + t0 + n]

    def wfc_t(dc, fc):
        return WB[:, dc * FF + fc * 128: dc * FF + fc * 128 + 128]

    def h_t(fc, n=512):
        return HB[:, fc * 512: fc * 512 + n]

    def wpe_t(par, fcl, dc):
        base = par * 4096 + fcl * 1024 + dc * 128
        return Q3[:, base: base + 128]

    def wpe_full(par):
        return Q3[:, par * 4096:(par + 1) * 4096]

    def xn_t(dc):
        return Q3[:, 8192 + dc * 512: 8192 + dc * 512 + 512]

    def xb_t(par):
        return Q3[:, 12288 + par * 512: 12288 + par * 512 + 512]

    def xq_t(par):
        return Q3[:, 13312 + par * 512: 13312 + par * 512 + 512]

    def hh_t(par):
        return Q3[:, 14336 + par * 512: 14336 + par * 512 + 512]

    def qs_t(pc, t0, n=512, p0=0, pn=128):
        return Q3[p0:p0 + pn, pc * S + t0: pc * S + t0 + n]

    def a_t(dc):
        return HB[:, dc * 512: dc * 512 + 512]

    def outT_t(pc, q0, n):
        return HB[:, pc * S + q0: pc * S + q0 + n]

    WQP, WOP, P0P, P1P, KTP, VBP, VTP, WKP, WVP, MSP, R2P = (
        0, 8192, 8192, 16384, 24576, 26624, 27648, 30720, 31744, 27648, 29696)

    def wq_t(dc, oc):
        return WB[:, WQP + dc * 1024 + oc * 128: WQP + dc * 1024 + oc * 128 + 128]

    def wo_t(pc, dc):
        return WB[:, WOP + pc * 1024 + dc * 128: WOP + pc * 1024 + dc * 128 + 128]

    def kt2_t(t0, n=512, p0=0, pn=128):
        return WB[p0:p0 + pn, KTP + t0: KTP + t0 + n]

    def vb_t(jc):
        return WB[:, VBP + jc * 64: VBP + jc * 64 + 64]

    def p2_t(par, pc, hl, n=512):
        base = (P0P if par == 0 else P1P) + (pc * 2 + hl) * n
        return WB[:, base: base + n]

    def wkk_t(dc):
        return WB[:, WKP + dc * 128: WKP + dc * 128 + 128]

    def wvv_t(dc):
        return WB[:, WVP + dc * 64: WVP + dc * 64 + 64]

    def vt_t(t0, n):
        return WB[0:64, VTP + t0: VTP + t0 + n]

    def msc_t(c0, n):
        return WB[:, MSP + c0: MSP + c0 + n]

    def r2_t(c0, n):
        return WB[0:2, R2P + c0: R2P + c0 + n]

    def scr_t(n=512, p0=0, pn=128):
        return WB[p0:p0 + pn, VTP: VTP + n]

    PB = [nc.alloc_psum_tensor(f"pb{i}", [128, 512], F32) for i in range(8)]

    P = Prog()

    for dc in range(DC):
        P.emit("S", lambda E, dc=dc: E.dma_start(
            out=xsl(dc, 0, S), in_=xin[dc * 128:(dc + 1) * 128, :]),
            inc=("g_x", 16))
    for h in range(2):
        for dc in range(DC):
            P.emit("G", lambda E, dc=dc, h=h: E.dma_start(
                out=WB[:, dc * FF + h * 2048: dc * FF + (h + 1) * 2048],
                in_=wfcT[dc * 128:(dc + 1) * 128, h * 2048:(h + 1) * 2048]),
                inc=("g_wfc" if h == 0 else "g_wfcb", 16))
    P.emit("G", lambda E: E.dma_start(out=gam[:], in_=gamd), inc=("g_misc", 16))
    P.emit("G", lambda E: E.dma_start(out=tri[:], in_=trid), inc=("g_tri", 16))

    P.emit("V", lambda E: E.memset(ones[:, 0:1], 1.0 / 1024.0))
    P.drain("V")
    P.emit("V", lambda E: E.memset(ones[:, 1:2], 1.0), inc=("v_on", 1))
    P.drain("V")

    def ln_stats(tag, with_t2, xwait=None):
        P.wait("V", "g_x", 16 * DC)
        P.wait("A", "g_x", 16 * DC)
        P.wait("T", "v_on", 1)
        for tb in range(NT):
            if xwait is not None:
                P.wait("V", xwait[0], xwait[1](tb))
            for dc in range(DC):
                par = dc % 2
                k = tb * DC + dc
                P.wait("V", "t_sx", max(0, P.val("t_sx") - 1))
                P.emit("V", lambda E, dc=dc, tb=tb, par=par: E.tensor_copy(
                    xb_t(par)[:], xsl(dc, tb * 512)), inc=("v_xb", 1))
                P.wait("A", "t_sx", max(0, P.val("t_sx") - 1))
                if xwait is not None:
                    P.wait("A", xwait[0], xwait[1](tb))
                P.emit("A", lambda E, par=par, dc=dc, tb=tb: E.activation(
                    xq_t(par)[:], xsl(dc, tb * 512), AF.Square), inc=("a_sq", 1))
                P.wait("T", "v_xb")
                P.emit("T", lambda E, par=par, dc=dc: E.matmul(
                    PB[6][0:1, :], ones[:, 0:1], xb_t(par)[:],
                    start=(dc == 0), stop=(dc == DC - 1)))
                P.wait("T", "a_sq")
                P.emit("T", lambda E, par=par, dc=dc: E.matmul(
                    PB[7][0:1, :], ones[:, 0:1], xq_t(par)[:],
                    start=(dc == 0), stop=(dc == DC - 1)),
                    inc=("t_sx", 1))
            P.wait("V", "t_sx")
            P.wait("V", "s_sg")
            P.emit("V", lambda E: E.tensor_copy(stg[0:1, 0:512], PB[6][0:1, :]))
            P.drain("V")
            P.emit("V", lambda E: E.tensor_copy(stg[0:1, 512:1024], PB[7][0:1, :]),
                   inc=("v_sg", 1))
            P.wait("S", "v_sg")
            P.emit("S", lambda E, tb=tb: E.dma_start(
                out=mu_d[0:1, tb * 512:tb * 512 + 512], in_=stg[0:1, 0:512]),
                inc=("s_sg", 16))
            P.emit("S", lambda E, tb=tb: E.dma_start(
                out=m2_d[0:1, tb * 512:tb * 512 + 512], in_=stg[0:1, 512:1024]),
                inc=("s_sg", 16))
            P.wait("T", "v_sg")

        P.wait("S", "s_sg")
        P.emit("S", lambda E: E.dma_start(
            out=foldv(0), in_=mu_d.rearrange("o (p n) -> (o p) n", p=128)),
            inc=("s_fl", 16))
        P.emit("S", lambda E: E.dma_start(
            out=foldv(1), in_=m2_d.rearrange("o (p n) -> (o p) n", p=128)),
            inc=("s_fl", 16))
        P.wait("V", "s_fl")
        P.emit("V", lambda E: E.tensor_mul(foldv(2), foldv(0), foldv(0)))
        P.drain("V")
        P.emit("V", lambda E: E.tensor_sub(foldv(2), foldv(1), foldv(2)))
        P.drain("V")
        P.emit("V", lambda E: E.tensor_scalar_add(foldv(3), foldv(2), EPS),
               inc=("v_u", 1))
        P.wait("A", "v_u")
        P.emit("A", lambda E: E.activation(foldv(4), foldv(3), AF.Sqrt),
               inc=("a_s0", 1))
        P.wait("V", "a_s0")
        P.emit("V", lambda E: E.reciprocal(foldv(5), foldv(4)))
        P.drain("V")
        P.emit("V", lambda E: E.tensor_mul(foldv(6), foldv(5), foldv(5)))
        P.drain("V")
        P.emit("V", lambda E: E.tensor_mul(foldv(6), foldv(6), foldv(3)))
        P.drain("V")
        P.emit("V", lambda E: E.tensor_scalar(foldv(6), foldv(6), -0.5, 1.5,
                                              ALU.mult, ALU.add))
        P.drain("V")
        P.emit("V", lambda E: E.tensor_mul(foldv(6), foldv(6), foldv(5)),
               inc=("v_r", 1))
        P.wait("S", "v_r")
        P.emit("S", lambda E: E.dma_start(
            out=r_d.rearrange("o (p n) -> (o p) n", p=128), in_=foldv(6)),
            inc=("s_r", 16))
        if with_t2:
            P.drain("V")
            P.emit("V", lambda E: E.reciprocal(foldv(2), foldv(3)))
            P.drain("V")
            P.emit("V", lambda E: E.tensor_scalar(foldv(2), foldv(2), -EPS,
                                                  1.0 + EPS, ALU.mult, ALU.add),
                   inc=("v_w", 1))
            P.wait("A", "v_w")
            P.emit("A", lambda E: E.activation(foldv(4), foldv(2), AF.Sqrt),
                   inc=("a_s1", 1))
            P.wait("V", "a_s1")
            P.emit("V", lambda E: E.reciprocal(foldv(5), foldv(4)))
            P.drain("V")
            P.emit("V", lambda E: E.tensor_mul(foldv(7), foldv(5), foldv(5)))
            P.drain("V")
            P.emit("V", lambda E: E.tensor_mul(foldv(7), foldv(7), foldv(2)))
            P.drain("V")
            P.emit("V", lambda E: E.tensor_scalar(foldv(7), foldv(7), -0.5, 1.5,
                                                  ALU.mult, ALU.add))
            P.drain("V")
            P.emit("V", lambda E: E.tensor_mul(foldv(7), foldv(7), foldv(5)))
            P.drain("V")
            P.emit("V", lambda E: E.tensor_scalar_mul(foldv(7), foldv(7), 0.125))
            P.drain("V")
            P.emit("V", lambda E: E.tensor_mul(foldv(7), foldv(7), foldv(6)),
                   inc=("v_t2", 1))
            P.wait("S", "v_t2")
            P.emit("S", lambda E: E.dma_start(
                out=t2_d.rearrange("o (p n) -> (o p) n", p=128), in_=foldv(7)),
                inc=("s_t2", 16))

    def ffn(tag, store_out=False):
        ln_stats(tag, with_t2=False,
                 xwait=("v_wo", lambda tb: (tb + 1) * DC) if tag == "f2" else None)
        tg0 = P.val("t_g")
        tye0 = P.val("t_ye")
        tyd0 = P.val("t_yd")
        vyc0 = P.val("v_yc")
        vsp0 = P.val("v_sp")

        def emit_spike(tb, dc):
            if dc == 0:
                P.wait("V", "t_ye", tye0 + (tb + 1) * DC)
            ysrc = (lambda: ysb[:, dc * 512:(dc + 1) * 512]) if dc < 2 \
                else (lambda: PB[dc][:])
            P.emit("V", lambda E, dc=dc: E.tensor_single_scalar(
                acc[:], ysrc(), THETA[0], ALU.is_ge))
            P.drain("V")
            for i in range(1, 5):
                P.emit("V", lambda E, dc=dc, i=i: E.scalar_tensor_tensor(
                    acc[:], ysrc(), THETA[i], acc[:], ALU.is_ge, ALU.add))
                P.drain("V")
            P.emit("V", lambda E, dc=dc, tb=tb: E.scalar_tensor_tensor(
                xsl(dc, tb * 512), acc[:], 0.1, xsl(dc, tb * 512),
                ALU.mult, ALU.add), inc=("v_sp", 1))
            P.drain("V")
            if store_out:
                P.wait("S", "v_sp")
                P.emit("S", lambda E, dc=dc, tb=tb: E.dma_start(
                    out=xout[dc * 128:(dc + 1) * 128, tb * 512:(tb + 1) * 512],
                    in_=xsl(dc, tb * 512)), inc=("s_out", 16))

        def emit_apply(tb):
            P.wait("S", "v_ap")
            P.emit("S", lambda E, tb=tb: E.dma_start(
                out=MU_t[:], in_=bass.AP(mu_d.tensor, tb * 512, [[0, 128], [1, 512]])),
                inc=("s_mu", 16))
            P.wait("S", "s_r")
            P.emit("S", lambda E, tb=tb: E.dma_start(
                out=R_t[:], in_=bass.AP(r_d.tensor, tb * 512, [[0, 128], [1, 512]])),
                inc=("s_mu", 16))
            P.wait("V", "s_mu")
            P.wait("V", "t_g", tg0 + tb * FCN)
            for dc in range(DC):
                P.emit("V", lambda E, dc=dc, tb=tb: E.tensor_sub(
                    acc[:], xsl(dc, tb * 512), MU_t[:]))
                P.drain("V")
                P.emit("V", lambda E, dc=dc: E.tensor_mul(
                    xn_t(dc)[:], acc[:], R_t[:]), inc=("v_ap", 1))
                P.drain("V")

        emit_apply(0)
        for tb in range(NT):
            if tag == "f1" and tb == 1:
                P.wait("A", "g_wfc", 16 * DC)
                P.wait("A", "g_wfcb", 16 * DC)
                for dc in range(DC):
                    P.emit("A", lambda E, dc=dc: E.dma_start(
                        out=wf16[dc * 128:(dc + 1) * 128, :],
                        in_=WB[:, dc * FF: dc * FF + FF]), inc=("s_wf16", 16))
            vap_tb = P.val("v_ap") - DC
            P.wait("T", "v_yc", vyc0 + tb)
            if tag == "f2":
                P.wait("T", "g_wfc2", 16 * DC)
            for fc in range(FCN):
                par = fc % 2
                if tag == "f1" and tb == 0 and fc in (0, 16):
                    P.wait("T", "g_wfc" if fc == 0 else "g_wfcb", 16 * DC)
                P.wait("T", "a_rl", max(0, P.val("a_rl") - 1))
                for dc in range(DC):
                    if fc == 0:
                        P.wait("T", "v_ap", vap_tb + dc + 1)
                    P.emit("T", lambda E, par=par, dc=dc, fc=fc: E.matmul(
                        PB[par][:], wfc_t(dc, fc), xn_t(dc)[:],
                        start=(dc == 0), stop=(dc == DC - 1)),
                        inc=("t_g", 1) if dc == DC - 1 else None)
                P.wait("A", "t_g")
                P.wait("A", "v_h", max(0, P.val("v_h") - 1))
                P.emit("A", lambda E, par=par: E.activation(
                    hh_t(par)[:], PB[par][:], AF.Relu), inc=("a_rl", 1))
                P.wait("V", "a_rl")
                P.emit("V", lambda E, par=par, fc=fc: E.tensor_mul(
                    h_t(fc), hh_t(par)[:], hh_t(par)[:]), inc=("v_h", 1))
                if tb > 0 and fc % 4 == 3:
                    emit_spike(tb - 1, fc // 4)
            for e in range(8):
                par = e % 2
                ge = tb * 8 + e
                P.wait("G", "t_ye", max(tye0, tye0 + ge - 1))
                if tag == "f1" and tb == 0 and e >= 2:
                    P.wait("G", f"s_wp16p{e % 2}", 16 * (e // 2))
                if tag == "f2" and tb == 0 and e < 2:
                    P.wait("G", "t_s")
                if tag == "f1" and tb == 0:
                    for fl in range(4):
                        P.emit("G", lambda E, e=e, par=par, fl=fl: E.dma_start(
                            out=Q3[:, par * 4096 + fl * 1024: par * 4096 + fl * 1024 + 1024],
                            in_=wprojT[e * 512 + fl * 128: e * 512 + fl * 128 + 128, :]),
                            inc=(f"g_wp{par}", 16))
                    P.wait("S", f"g_wp{par}")
                    P.emit("S", lambda E, e=e, par=par: E.dma_start(
                        out=wp16[e * 128:(e + 1) * 128, :], in_=wpe_full(par)),
                        inc=(f"s_wp16p{par}", 16))
                else:
                    if tag == "f1" and tb == 1 and e < 2:
                        P.wait("G", f"s_wp16p{e % 2}", 64)
                    P.emit("G", lambda E, e=e, par=par: E.dma_start(
                        out=wpe_full(par), in_=wp16[e * 128:(e + 1) * 128, :]),
                        inc=(f"g_wp{par}", 16))
                P.wait("T", f"g_wp{par}")
                if e == 0:
                    P.wait("T", "v_h")
                for dc in range(DC):
                    if e == 0 and tb > 0:
                        P.wait("T", "v_sp", vsp0 + (tb - 1) * DC + dc + 1)
                    for fcl in range(4):
                        fc = e * 4 + fcl
                        st = (e == 0 and fcl == 0)
                        sp = (e == 7 and fcl == 3)
                        if dc == DC - 1 and fcl == 3:
                            mark = ("t_ye", 1)
                        elif e == 7 and fcl == 3 and dc < 2:
                            mark = ("t_yd", 1)
                        else:
                            mark = None
                        P.emit("T", lambda E, dc=dc, par=par, fcl=fcl, fc=fc,
                               st=st, sp=sp: E.matmul(
                            PB[dc][:], wpe_t(par, fcl, dc), h_t(fc),
                            start=st, stop=sp), inc=mark)
            if tb + 1 < NT:
                emit_apply(tb + 1)
            P.wait("V", "t_yd", tyd0 + (tb + 1) * 2)
            P.emit("V", lambda E: E.tensor_copy(ysb[:, 0:512], PB[0][:]))
            P.drain("V")
            P.emit("V", lambda E: E.tensor_copy(ysb[:, 512:1024], PB[1][:]),
                   inc=("v_yc", 1))
            P.drain("V")
            if tb == NT - 1:
                for dc in range(DC):
                    emit_spike(tb, dc)

    def attention():
        ln_stats("a", with_t2=True)
        P.wait("G", "t_g")
        P.wait("G", "t_ye")
        P.wait("G", "s_wf16", 16 * DC)
        for dc in range(DC):
            P.emit("G", lambda E, dc=dc: E.dma_start(
                out=WB[:, WQP + dc * 1024: WQP + dc * 1024 + 1024],
                in_=wq[dc * 128:(dc + 1) * 128, :]), inc=("g_wq", 16))
        for dc in range(DC):
            P.emit("G", lambda E, dc=dc: E.dma_start(
                out=WB[:, WKP + dc * 128: WKP + dc * 128 + 128],
                in_=wkk[dc * 128:(dc + 1) * 128, :]), inc=("g_wk", 16))
            P.emit("G", lambda E, dc=dc: E.dma_start(
                out=WB[:, WVP + dc * 64: WVP + dc * 64 + 64],
                in_=wvv[dc * 128:(dc + 1) * 128, :]), inc=("g_wk", 16))
        P.wait("V", "g_wq", 16 * DC)
        P.wait("V", "g_misc", 16)
        for dc in range(DC):
            P.emit("V", lambda E, dc=dc: E.tensor_scalar_mul(
                WB[:, WQP + dc * 1024: WQP + dc * 1024 + 1024],
                WB[:, WQP + dc * 1024: WQP + dc * 1024 + 1024],
                gam[:, dc:dc + 1]), inc=("v_gw", 1))
            P.drain("V")

        P.wait("V", "s_wp16p0", P.val("s_wp16p0"))
        P.wait("V", "s_wp16p1", P.val("s_wp16p1"))
        vapa0 = P.val("v_apa")
        for tb in range(NT):
            P.wait("S", "v_apa")
            P.wait("S", "v_qs")
            P.emit("S", lambda E, tb=tb: E.dma_start(
                out=MU_t[:], in_=bass.AP(mu_d.tensor, tb * 512, [[0, 128], [1, 512]])),
                inc=("s_muo", 16))
            P.wait("S", "s_r")
            P.emit("S", lambda E, tb=tb: E.dma_start(
                out=R_t[:], in_=bass.AP(r_d.tensor, tb * 512, [[0, 128], [1, 512]])),
                inc=("s_mua", 16))
            P.wait("S", "s_t2")
            P.emit("S", lambda E, tb=tb: E.dma_start(
                out=T2_t, in_=bass.AP(t2_d.tensor, tb * 512, [[0, 128], [1, 512]])),
                inc=("s_mua", 16))
            P.wait("V", "s_muo", 16 * (tb + 1))
            P.wait("V", "v_gw", DC)
            P.wait("V", "t_kv", tb)
            P.wait("V", "t_vt", tb)
            P.wait("V", "t_qs", tb * DC)
            for dc in range(DC):
                P.emit("V", lambda E, dc=dc, tb=tb: E.tensor_sub(
                    a_t(dc)[:], xsl(dc, tb * 512), MU_t[:]), inc=("v_apa", 1))
                P.drain("V")
            P.wait("T", "g_wk", 16 * 2 * DC)
            P.wait("T", "v_kv", tb)
            for dc in range(DC):
                P.wait("T", "v_apa", vapa0 + tb * DC + dc + 1)
                P.emit("T", lambda E, dc=dc: E.matmul(
                    PB[3][:], wkk_t(dc), a_t(dc)[:],
                    start=(dc == 0), stop=(dc == DC - 1)),
                    inc=("t_kv", 1) if dc == DC - 1 else None)
            P.wait("T", "v_vt", tb)
            for dc in range(DC):
                P.emit("T", lambda E, dc=dc: E.matmul(
                    PB[4][0:64, :], wvv_t(dc), a_t(dc)[:],
                    start=(dc == 0), stop=(dc == DC - 1)),
                    inc=("t_vt", 1) if dc == DC - 1 else None)
            P.wait("V", "s_mua")
            P.wait("V", "t_kv")
            P.emit("V", lambda E, tb=tb: E.tensor_mul(
                kt2_t(tb * 512), PB[3][:], R_t[:]), inc=("v_kv", 1))
            P.drain("V")
            P.wait("V", "t_vt")
            P.emit("V", lambda E, tb=tb: E.tensor_mul(
                vt_t(tb * 512, 512), PB[4][0:64, :], R_t[0:64, :]),
                inc=("v_vt", 1))
            P.drain("V")
            for oc in range(DC):
                bank = 5 + (oc % 2)
                P.wait("T", "v_qs", max(0, P.val("v_qs") - 1))
                for dc in range(DC):
                    P.emit("T", lambda E, oc=oc, dc=dc, bank=bank: E.matmul(
                        PB[bank][:], wq_t(dc, oc), a_t(dc)[:],
                        start=(dc == 0), stop=(dc == DC - 1)),
                        inc=("t_qs", 1) if dc == DC - 1 else None)
                P.wait("V", "t_qs")
                P.emit("V", lambda E, oc=oc, tb=tb, bank=bank: E.tensor_mul(
                    qs_t(oc, tb * 512), PB[bank][:], T2_t), inc=("v_qs", 1))
                P.drain("V")
                P.emit("V", lambda E, oc=oc, tb=tb: E.tensor_mul(
                    WB[:, 8192:8704], qs_t(oc, tb * 512), qs_t(oc, tb * 512)),
                    inc=("v_qn2", 1))
                P.drain("V")
                P.wait("T", "v_qn2")
                P.wait("T", "v_qn")
                P.emit("T", lambda E: E.matmul(
                    PB[7][0:1, :], ones[:, 1:2], WB[:, 8192:8704],
                    start=True, stop=True), inc=("t_qn", 1))
                P.wait("V", "t_qn")
                P.emit("V", lambda E, oc=oc, tb=tb: E.reduce_max(
                    crow[0:1, oc * 4 + tb: oc * 4 + tb + 1], PB[7][0:1, :],
                    axis=mybir.AxisListType.X), inc=("v_qn", 1))
                P.drain("V")

        P.wait("S", "v_apa")
        P.emit("S", lambda E: E.dma_start(out=ident, in_=identd), inc=("s_tri", 16))
        P.wait("S", "v_vt")
        P.emit("S", lambda E: E.dma_start_transpose(
            WB[:, VBP:VBP + S // 2].rearrange("p (c n) -> p c n", n=64),
            WB[0:64, VTP:VTP + S].rearrange("p (c n) -> p c n", n=128)),
            inc=("s_vb", 16))

        P.wait("A", "t_qn")
        P.wait("A", "s_vb", 16)
        P.wait("T", "v_qn")
        for tt in range(NT):
            P.wait("A", "v_kv")
            P.wait("A", "t_kn")
            P.emit("A", lambda E, tt=tt: E.activation(
                scr_t(512, 0, 64), kt2_t(tt * 512, 512, 0, 64), AF.Square),
                inc=("a_kn", 1))
            P.wait("T", "a_kn")
            P.wait("T", "v_kn")
            P.emit("T", lambda E: E.matmul(
                PB[7][0:1, :], ones[0:64, 1:2], scr_t(512, 0, 64),
                start=True, stop=True), inc=("t_kn", 1))
            P.wait("V", "t_kn")
            P.emit("V", lambda E, tt=tt: E.reduce_max(
                crow[0:1, 32 + tt: 33 + tt], PB[7][0:1, :],
                axis=mybir.AxisListType.X), inc=("v_kn", 1))
        P.wait("V", "v_qn")
        P.wait("V", "v_kn")
        for pc in range(NP):
            P.emit("V", lambda E, pc=pc: E.reduce_max(
                crow[0:1, 40 + pc: 41 + pc], crow[0:1, pc * 4: pc * 4 + NT],
                axis=mybir.AxisListType.X))
            P.drain("V")
        P.emit("V", lambda E: E.reduce_max(
            crow[0:1, 48:49], crow[0:1, 32: 32 + NT], axis=mybir.AxisListType.X))
        P.drain("V")
        P.emit("V", lambda E: E.tensor_single_scalar(
            crow[0:1, 40:48], crow[0:1, 40:48], crow[0:1, 48:49], ALU.mult),
            inc=("v_c1", 1))
        P.wait("A", "v_c1")
        P.emit("A", lambda E: E.activation(
            crow[0:1, 56:64], crow[0:1, 40:48], AF.Sqrt), inc=("a_c", 1))
        P.wait("V", "a_c")
        P.emit("V", lambda E: E.tensor_scalar_mul(
            crow[0:1, 56:64], crow[0:1, 56:64], -1.0), inc=("v_c2", 1))
        P.wait("S", "v_c2")
        P.emit("S", lambda E: E.dma_start(out=nc_d, in_=crow[0:1, 56:64]),
               inc=("s_c", 16))
        P.wait("S", "s_c")
        P.emit("S", lambda E: E.dma_start(
            out=ncb[:], in_=bass.AP(nc_d.tensor, 0, [[0, 128], [1, 8]])),
            inc=("s_c", 16))

        P.wait("T", "s_vb", 16)
        P.wait("T", "v_kv")
        P.wait("T", "v_vt")
        P.wait("T", "v_qs")
        P.wait("A", "s_c", 32)
        P.wait("A", "s_tri", 16)
        P.wait("T", "s_tri", 16)
        P.wait("V", "g_tri", 16)
        P.wait("S", "v_gw", DC)
        P.emit("G", lambda E: E.dma_start(out=e2[:], in_=e2d), inc=("g_e2", 16))
        P.wait("T", "g_e2", 16)

        NQT = S // 128
        tr_done = {}
        ex_done = {}
        unit_of = {}
        uidx = 0
        pend_av = None
        pend_msc = None

        def emit_av(qt, jb, u):
            ext = (qt + 1) * 128
            jbn = (ext + 511) // 512
            N = min(512, ext - jb * 512)
            jcn = N // 128
            ob = 4 if qt % 2 == 0 else 6
            P.wait("T", "s_tr", tr_done[u])
            P.wait("T", "t_av", u)
            if jb == 0:
                P.wait("T", "v_msc", max(0, 2 * (qt - 1)))
            for jcl in range(jcn):
                jc = jb * 4 + jcl
                st = (jb == 0 and jcl == 0)
                sp = (jb == jbn - 1 and jcl == jcn - 1)
                for hl in range(2):
                    for half in range(2):
                        lastmm = (jcl == jcn - 1 and hl == 1 and half == 1)
                        off = ((half * 8 + hl) * jcn + jcl) * 128
                        rhs = bass.AP(WB, off, [[WB.shape[1], 128],
                                                [2 * jcn * 128, 4], [1, 128]])
                        P.emit("T", lambda E, jc=jc, rhs=rhs, hl=hl, half=half,
                               st=st, sp=sp, ob=ob: E.matmul(
                                   PB[ob + half][hl * 64:(hl + 1) * 64, 0:512],
                                   vb_t(jc), rhs,
                                   start=st, stop=sp, skip_group_check=True,
                                   tile_position=(0, hl * 64)),
                               inc=("t_av", 1) if lastmm else None)

        def emit_msc(qt):
            ob = 4 if qt % 2 == 0 else 6
            spb = 2 * (1 - (P.val("t_s") // 2) % 2)
            P.wait("V", "a_ex", ex_done[qt])
            po = (qt % 2) * 64
            P.emit("V", lambda E, po=po: E.tensor_reduce(
                rs16[:], sacc[:, po:po + 64].rearrange("p (g j) -> p g j", j=4),
                axis=mybir.AxisListType.X, op=ALU.add))
            P.drain("V")
            P.emit("V", lambda E: E.reciprocal(rs16[:], rs16[:]),
                   inc=("v_rs", 1))
            P.drain("V")
            P.wait("T", "v_rs", qt + 1)
            P.wait("T", "a_ex")
            for pc in range(NP):
                P.emit("T", lambda E, pc=pc, mb=spb: E.transpose(
                    PB[mb + pc // 4][0:2, (pc % 4) * 128:(pc % 4) * 128 + 128],
                    rs16[:, pc * 2: pc * 2 + 2], ident),
                    inc=("t_mst", 1) if pc == NP - 1 else None)
            P.wait("V", "t_mst", qt + 1)
            P.emit("V", lambda E, mb=spb: E.tensor_copy(
                r2_t(0, 512), PB[mb][0:2, 0:512]))
            P.drain("V")
            P.emit("V", lambda E, mb=spb: E.tensor_copy(
                r2_t(512, 512), PB[mb + 1][0:2, 0:512]), inc=("v_r2", 1))
            P.wait("T", "v_r2", qt + 1)
            for half in range(2):
                P.emit("T", lambda E, half=half, mb=spb: E.matmul(
                    PB[mb + half][:, 0:512], e2[:], r2_t(half * 512, 512),
                    start=True, stop=True), inc=("t_msm", 1) if half else None)
            P.wait("V", "t_msm", qt + 1)
            for half in range(2):
                P.emit("V", lambda E, half=half, mb=spb: E.tensor_copy(
                    msc_t(half * 512, 512), PB[mb + half][:, 0:512]))
                P.drain("V")
            P.emit("V", lambda E: E.engine_nop(), inc=("v_mc", 1))
            P.wait("V", "t_av", unit_of[(qt, ((qt + 1) * 128 + 511) // 512 - 1)] + 1)
            for half in range(2):
                dstbase = half * 4 * S + qt * 128
                dst3 = bass.AP(HB, dstbase, [[HB.shape[1], 128], [S, 4], [1, 128]])
                P.emit("V", lambda E, half=half, ob=ob, dst3=dst3: E.tensor_tensor(
                    dst3,
                    PB[ob + half][:, 0:512].rearrange("p (pc n) -> p pc n", n=128),
                    msc_t(half * 512, 512).rearrange("p (pc n) -> p pc n", n=128),
                    op=ALU.mult), inc=("v_msc", 1))
                P.drain("V")

        for qt in range(NQT):
            ext = (qt + 1) * 128
            jbn = (ext + 511) // 512
            par = qt % 2
            po = par * 64
            if pend_msc is not None:
                emit_msc(pend_msc)
                pend_msc = None
            P.wait("V", "a_ex")
            P.emit("V", lambda E, po=po: E.memset(sacc[:, po:po + 64], 0.0),
                   inc=("v_sz", 1))
            for jb in range(jbn):
                u = uidx
                uidx += 1
                unit_of[(qt, jb)] = u
                upar = u % 2
                N = min(512, ext - jb * 512)
                for pc in range(NP):
                    i = P.val("t_s") // 2
                    bA = (i % 2) * 2
                    bB = bA + 1
                    P.wait("T", "a_ex", max(0, P.val("a_ex") - 2))
                    P.wait("T", "v_mc")
                    P.emit("T", lambda E, pc=pc, qt=qt, jb=jb, N=N, bA=bA:
                           E.matmul(PB[bA][:, 0:N],
                                    qs_t(pc, qt * 128, 128, 0, 64),
                                    kt2_t(jb * 512, N, 0, 64),
                                    start=True, stop=True))
                    P.emit("T", lambda E, pc=pc, qt=qt, jb=jb, N=N, bB=bB:
                           E.matmul(PB[bB][:, 0:N],
                                    qs_t(pc, qt * 128, 128, 64, 64),
                                    kt2_t(jb * 512, N, 64, 64),
                                    start=True, stop=True), inc=("t_s", 2))
                    diag = qt * 128 - jb * 512
                    if 0 <= diag < N:
                        P.wait("V", "t_s")
                        for b in (bA, bB):
                            P.emit("V", lambda E, b=b, diag=diag: E.tensor_add(
                                PB[b][:, diag:diag + 128],
                                PB[b][:, diag:diag + 128], tri[:]),
                                inc=("v_msk", 1))
                            P.drain("V")
                    if 0 <= diag < N:
                        P.wait("A", "v_msk")
                    P.wait("A", "t_s")
                    if jb == 0 and pc == 0:
                        P.wait("A", "v_sz")
                    if u >= 2 and pc == 0:
                        P.wait("A", "s_tr", tr_done[u - 2])
                    for hl in range(2):
                        g = pc * 2 + hl
                        P.emit("A", lambda E, hl=hl, par=upar, jb=jb, N=N,
                               b=(bA, bB)[hl], pc=pc, g=g, po=po: E.activation(
                                   p2_t(par, pc, hl, N), PB[b][:, 0:N],
                                   AF.Exp, bias=ncb[:, pc:pc + 1], scale=1.0,
                                   accum_out=sacc[:, po + g * 4 + jb:
                                                  po + g * 4 + jb + 1]),
                               inc=("a_ex", 1))
                P.wait("S", "a_ex")
                P.wait("S", "t_av", u)
                jcn = N // 128
                srcbase = P0P if upar == 0 else P1P
                src2 = bass.AP(WB, srcbase, [[WB.shape[1], 128], [1, 16 * N]])
                dst3 = bass.AP(WB, 0,
                               [[WB.shape[1], 128], [128, 16 * jcn], [1, 128]])
                P.emit("S", lambda E, dst3=dst3, src2=src2:
                       E.dma_start_transpose(dst3, src2), inc=("s_tr", 16))
                tr_done[u] = P.val("s_tr")
                if pend_msc is not None:
                    emit_msc(pend_msc)
                    pend_msc = None
                if pend_av is not None:
                    pqt, pjb, pu = pend_av
                    emit_av(pqt, pjb, pu)
                    pext = (pqt + 1) * 128
                    if pjb == (pext + 511) // 512 - 1:
                        pend_msc = pqt
                pend_av = (qt, jb, u)
            ex_done[qt] = P.val("a_ex")
        if pend_msc is not None:
            emit_msc(pend_msc)
            pend_msc = None
        pqt, pjb, pu = pend_av
        emit_av(pqt, pjb, pu)
        emit_msc(pqt)

        P.wait("G", "a_ex")
        P.wait("G", "s_tr")
        for pc in range(NP):
            P.emit("G", lambda E, pc=pc: E.dma_start(
                out=WB[:, WOP + pc * 1024: WOP + pc * 1024 + 1024],
                in_=wo[pc * 128:(pc + 1) * 128, :]), inc=("g_wo", 16))
        if stage == "attn":
            otd = nc.dram_tensor("otd", [128, 16384], BF16, kind="ExternalOutput").ap()
            P.wait("S", "v_msc")
            P.emit("S", lambda E: E.dma_start(out=otd, in_=HB[:]), inc=("s_out", 16))
        P.wait("T", "v_msc")
        P.wait("T", "g_wo", 16 * NP)
        for tt in range(NT):
            for dc in range(DC):
                bank = (tt * DC + dc) % 2
                P.wait("T", "v_wo", max(0, P.val("v_wo") - 1))
                for pc in range(NP):
                    P.emit("T", lambda E, pc=pc, dc=dc, tt=tt, bank=bank: E.matmul(
                        PB[bank][:], wo_t(pc, dc), outT_t(pc, tt * 512, 512),
                        start=(pc == 0), stop=(pc == NP - 1)),
                        inc=("t_wo", 1) if pc == NP - 1 else None)
                P.wait("V", "t_wo")
                P.emit("V", lambda E, dc=dc, tt=tt, bank=bank: E.tensor_add(
                    xsl(dc, tt * 512), PB[bank][:], xsl(dc, tt * 512)),
                    inc=("v_wo", 1))
                P.drain("V")

        P.wait("G", "t_av")
        for dc in (0, 1, 4, 5):
            P.emit("G", lambda E, dc=dc: E.dma_start(
                out=WB[:, dc * FF: dc * FF + FF], in_=wf16[dc * 128:(dc + 1) * 128, :]),
                inc=("g_wfc2", 16))
        P.wait("G", "v_msc")
        P.wait("G", "t_msm")
        for dc in (6, 7):
            P.emit("G", lambda E, dc=dc: E.dma_start(
                out=WB[:, dc * FF: dc * FF + FF], in_=wf16[dc * 128:(dc + 1) * 128, :]),
                inc=("g_wfc2", 16))
        P.wait("G", "t_wo")
        for dc in (2, 3):
            P.emit("G", lambda E, dc=dc: E.dma_start(
                out=WB[:, dc * FF: dc * FF + FF], in_=wf16[dc * 128:(dc + 1) * 128, :]),
                inc=("g_wfc2", 16))


    ffn("f1", store_out=(stage == "ffn1"))
    if stage != "ffn1":
        attention()
        if stage == "attn":
            P.wait("S", "v_wo")
            for dc in range(DC):
                P.emit("S", lambda E, dc=dc: E.dma_start(
                    out=xout[dc * 128:(dc + 1) * 128, :], in_=xsl(dc, 0, S)),
                    inc=("s_out", 16))
        else:
            for e in ("V", "A", "T", "G"):
                P.wait(e, "t_s")
                P.wait(e, "t_av")
                P.wait(e, "a_ex")
                P.wait(e, "t_mst")
            ffn("f2", store_out=True)
    P.wait("S", "s_out")
    for name in list(P.sems):
        if name.startswith("s_"):
            P.wait("S", name)
        elif name.startswith("g_"):
            P.wait("G", name)

    semnames = sorted(P.sems.keys())
    with contextlib.ExitStack() as stack:
        sems = {n: stack.enter_context(nc.semaphore(n)) for n in semnames}
        with nc.Block() as block:
            @block.tensor
            def _(E):
                for fn in P.streams["T"]:
                    fn(E, sems)

            @block.vector
            def _(E):
                for fn in P.streams["V"]:
                    fn(E, sems)

            @block.scalar
            def _(E):
                for fn in P.streams["A"]:
                    fn(E, sems)

            @block.sync
            def _(E):
                for fn in P.streams["S"]:
                    fn(E, sems)

            @block.gpsimd
            def _(E):
                for fn in P.streams["G"]:
                    fn(E, sems)
    return nc


_cache = {}


def _get_program(S=2048, stage="full"):
    key = (S, stage)
    if key not in _cache:
        _cache[key] = build_program(S, stage)
    return _cache[key]


def make_inputs(L0_attn_x, L0_attn_gamma, L0_attn_Wq, L0_attn_Wkv, L0_attn_Wo,
                L0_ffn_W_fc, L0_ffn_W_proj):
    B = L0_attn_x.shape[0]
    wfcT_h = np.ascontiguousarray(np.asarray(L0_ffn_W_fc, np.float32).T)
    wprojT_h = np.ascontiguousarray(np.asarray(L0_ffn_W_proj, np.float32).T)
    wq_h = np.ascontiguousarray(np.asarray(L0_attn_Wq, np.float32))
    wkv = np.asarray(L0_attn_Wkv, np.float32)
    wkk_h = np.ascontiguousarray(np.concatenate([wkv[:, 0:DH], wkv[:, 0:DH]], axis=1))
    wvv_h = np.ascontiguousarray(wkv[:, DH:2 * DH])
    wo_h = np.ascontiguousarray(np.asarray(L0_attn_Wo, np.float32))
    gam_h = np.ascontiguousarray(
        np.asarray(L0_attn_gamma, np.float32).reshape(DC, 128).T)
    tri_h = np.where(np.arange(128)[None, :] > np.arange(128)[:, None],
                     np.float32(NEGBIG), np.float32(0.0)).astype(np.float32)
    ident_h = np.eye(128, dtype=np.float32)
    e2_h = np.zeros((2, 128), np.float32)
    e2_h[0, 0:64] = 1.0
    e2_h[1, 64:128] = 1.0
    maps = []
    for b in range(B):
        xb = np.ascontiguousarray(np.asarray(L0_attn_x[b], np.float32).T)
        maps.append({
            "xin": xb, "wfcT": wfcT_h, "wprojT": wprojT_h, "wq": wq_h,
            "wkk": wkk_h, "wvv": wvv_h, "wo": wo_h, "gamd": gam_h,
            "trid": tri_h, "identd": ident_h, "e2d": e2_h,
        })
    return maps


def kernel(L0_attn_x, L0_attn_gamma, L0_attn_Wq, L0_attn_Wkv, L0_attn_Wo,
           L0_ffn_W_fc, L0_ffn_W_proj):
    B, S, _ = L0_attn_x.shape
    maps = make_inputs(L0_attn_x, L0_attn_gamma, L0_attn_Wq, L0_attn_Wkv,
                       L0_attn_Wo, L0_ffn_W_fc, L0_ffn_W_proj)
    nc_prog = _get_program(S, "full")
    res = run_bass_kernel_spmd(nc_prog, maps, core_ids=list(range(B)))
    out = np.stack([np.asarray(r["xout"], np.float32).T for r in res.results],
                   axis=0)
    return np.ascontiguousarray(out)


# version 41
# speedup vs baseline: 1.1970x; 1.0091x over previous
import contextlib
import numpy as np
import concourse.bass as bass
import concourse.mybir as mybir
from concourse.bass_utils import run_bass_kernel_spmd

F32 = mybir.dt.float32
BF16 = mybir.dt.bfloat16
AF = mybir.ActivationFunctionType
ALU = mybir.AluOpType

THETA = (0.24419431388378143, 0.4639692008495331, 0.6861860156059265,
         0.9059608578681946, 1.083978533744812)

D = 1024
DC = 8
FF = 4096
FCN = 32
H = 16
DH = 64
NP = 8
EPS = 1e-5
NEGBIG = -1.0e30

ENGS = ("T", "V", "A", "S", "G")


class Prog:

    def __init__(self):
        self.streams = {e: [] for e in ENGS}
        self.sems = {}

    def inc(self, name, n=1):
        self.sems[name] = self.sems.get(name, 0) + n
        return self.sems[name]

    def val(self, name):
        return self.sems.get(name, 0)

    def emit(self, eng, fn, inc=None):
        if inc is not None:
            self.inc(inc[0], inc[1])

        def run(E, SEM, fn=fn, inc=inc):
            ins = fn(E)
            if inc is not None:
                ins.then_inc(SEM[inc[0]], inc[1])
        self.streams[eng].append(run)

    def wait(self, eng, name, val=None):
        v = self.val(name) if val is None else val
        if v <= 0:
            return
        self.streams[eng].append(
            lambda E, SEM, name=name, v=v: E.wait_ge(SEM[name], v))

    def drain(self, eng):
        self.streams[eng].append(lambda E, SEM: E.drain())


def build_program(S=2048, stage="full"):
    NT = S // 512
    NQB = S // 256
    nc = bass.Bass("TRN2", target_bir_lowering=False, debug=False)

    xin = nc.dram_tensor("xin", [D, S], F32, kind="ExternalInput").ap()
    wfcT = nc.dram_tensor("wfcT", [D, FF], F32, kind="ExternalInput").ap()
    wprojT = nc.dram_tensor("wprojT", [FF, D], F32, kind="ExternalInput").ap()
    wq = nc.dram_tensor("wq", [D, D], F32, kind="ExternalInput").ap()
    wkk = nc.dram_tensor("wkk", [D, 128], F32, kind="ExternalInput").ap()
    wvv = nc.dram_tensor("wvv", [D, DH], F32, kind="ExternalInput").ap()
    wo = nc.dram_tensor("wo", [D, D], F32, kind="ExternalInput").ap()
    gamd = nc.dram_tensor("gamd", [128, 8], F32, kind="ExternalInput").ap()
    trid = nc.dram_tensor("trid", [128, 128], F32, kind="ExternalInput").ap()
    identd = nc.dram_tensor("identd", [128, 128], F32, kind="ExternalInput").ap()
    e2d = nc.dram_tensor("e2d", [2, 128], F32, kind="ExternalInput").ap()
    xout = nc.dram_tensor("xout", [D, S], F32, kind="ExternalOutput").ap()
    mu_d = nc.dram_tensor("mu_d", [1, S], F32, kind="Internal").ap()
    m2_d = nc.dram_tensor("m2_d", [1, S], F32, kind="Internal").ap()
    r_d = nc.dram_tensor("r_d", [1, S], F32, kind="Internal").ap()
    t2_d = nc.dram_tensor("t2_d", [1, S], F32, kind="Internal").ap()
    nc_d = nc.dram_tensor("nc_d", [1, 8], F32, kind="Internal").ap()
    wp16 = nc.dram_tensor("wp16", [8 * 128, 4096], BF16, kind="Internal").ap()
    wf16 = nc.dram_tensor("wf16", [DC * 128, FF], BF16, kind="Internal").ap()

    X = nc.alloc_sbuf_tensor("X", [128, DC * S], F32)
    WB = nc.alloc_sbuf_tensor("WB", [128, 32768], BF16)
    HB = nc.alloc_sbuf_tensor("HB", [128, 16384], BF16)
    Q3 = nc.alloc_sbuf_tensor("Q3", [128, 16384], BF16)
    MU_t = nc.alloc_sbuf_tensor("MU_t", [128, 512], F32)
    R_t = nc.alloc_sbuf_tensor("R_t", [128, 512], F32)
    stg = nc.alloc_sbuf_tensor("stg", [1, 1024], F32)
    FJ = S // 128
    acc = nc.alloc_sbuf_tensor("acc", [128, 512], F32)
    ident = acc[:, 0:128]
    ysb = nc.alloc_sbuf_tensor("ysb", [128, 1024], F32)
    T2_t = ysb[:, 0:512]
    fold = ysb[:, 512:512 + 8 * FJ]
    sacc = nc.alloc_sbuf_tensor("sacc", [128, 128], F32)
    rs16 = nc.alloc_sbuf_tensor("rs16", [128, 16], F32)
    e2 = nc.alloc_sbuf_tensor("e2", [2, 128], BF16)
    ncb = nc.alloc_sbuf_tensor("ncb", [128, 8], F32)
    tri = nc.alloc_sbuf_tensor("tri", [128, 128], BF16)
    gam = nc.alloc_sbuf_tensor("gam", [128, 8], F32)
    ones = nc.alloc_sbuf_tensor("ones", [128, 2], BF16)
    crow = nc.alloc_sbuf_tensor("crow", [1, 64], F32)

    def foldv(i):
        return fold[:, i * FJ:(i + 1) * FJ]

    def xsl(dc, t0, n=512):
        return X[:, dc * S + t0: dc * S + t0 + n]

    def wfc_t(dc, fc):
        return WB[:, dc * FF + fc * 128: dc * FF + fc * 128 + 128]

    def h_t(fc, n=512):
        return HB[:, fc * 512: fc * 512 + n]

    def wpe_t(par, fcl, dc):
        base = par * 4096 + fcl * 1024 + dc * 128
        return Q3[:, base: base + 128]

    def wpe_full(par):
        return Q3[:, par * 4096:(par + 1) * 4096]

    def xn_t(dc):
        return Q3[:, 8192 + dc * 512: 8192 + dc * 512 + 512]

    def xb_t(par):
        return Q3[:, 12288 + par * 512: 12288 + par * 512 + 512]

    def xq_t(par):
        return Q3[:, 13312 + par * 512: 13312 + par * 512 + 512]

    def hh_t(par):
        return Q3[:, 14336 + par * 512: 14336 + par * 512 + 512]

    def qs_t(pc, t0, n=512, p0=0, pn=128):
        return Q3[p0:p0 + pn, pc * S + t0: pc * S + t0 + n]

    def a_t(dc):
        return HB[:, dc * 512: dc * 512 + 512]

    def outT_t(pc, q0, n):
        return HB[:, pc * S + q0: pc * S + q0 + n]

    WQP, WOP, P0P, P1P, KTP, VBP, VTP, WKP, WVP, MSP, R2P = (
        0, 8192, 8192, 16384, 24576, 26624, 27648, 30720, 31744, 27648, 29696)

    def wq_t(dc, oc):
        return WB[:, WQP + dc * 1024 + oc * 128: WQP + dc * 1024 + oc * 128 + 128]

    def wo_t(pc, dc):
        return WB[:, WOP + pc * 1024 + dc * 128: WOP + pc * 1024 + dc * 128 + 128]

    def kt2_t(t0, n=512, p0=0, pn=128):
        return WB[p0:p0 + pn, KTP + t0: KTP + t0 + n]

    def vb_t(jc):
        return WB[:, VBP + jc * 64: VBP + jc * 64 + 64]

    def p2_t(par, pc, hl, n=512):
        base = (P0P if par == 0 else P1P) + (pc * 2 + hl) * n
        return WB[:, base: base + n]

    def wkk_t(dc):
        return WB[:, WKP + dc * 128: WKP + dc * 128 + 128]

    def wvv_t(dc):
        return WB[:, WVP + dc * 64: WVP + dc * 64 + 64]

    def vt_t(t0, n):
        return WB[0:64, VTP + t0: VTP + t0 + n]

    def msc_t(c0, n):
        return WB[:, MSP + c0: MSP + c0 + n]

    def r2_t(c0, n):
        return WB[0:2, R2P + c0: R2P + c0 + n]

    def scr_t(n=512, p0=0, pn=128):
        return WB[p0:p0 + pn, VTP: VTP + n]

    PB = [nc.alloc_psum_tensor(f"pb{i}", [128, 512], F32) for i in range(8)]

    P = Prog()

    for dc in range(DC):
        P.emit("S", lambda E, dc=dc: E.dma_start(
            out=xsl(dc, 0, S), in_=xin[dc * 128:(dc + 1) * 128, :]),
            inc=("g_x", 16))
    for h in range(2):
        for dc in range(DC):
            P.emit("G", lambda E, dc=dc, h=h: E.dma_start(
                out=WB[:, dc * FF + h * 2048: dc * FF + (h + 1) * 2048],
                in_=wfcT[dc * 128:(dc + 1) * 128, h * 2048:(h + 1) * 2048]),
                inc=("g_wfc" if h == 0 else "g_wfcb", 16))
    P.emit("G", lambda E: E.dma_start(out=gam[:], in_=gamd), inc=("g_misc", 16))
    P.emit("G", lambda E: E.dma_start(out=tri[:], in_=trid), inc=("g_tri", 16))

    P.emit("V", lambda E: E.memset(ones[:, 0:1], 1.0 / 1024.0))
    P.drain("V")
    P.emit("V", lambda E: E.memset(ones[:, 1:2], 1.0), inc=("v_on", 1))
    P.drain("V")

    def ln_stats(tag, with_t2, xwait=None):
        P.wait("V", "g_x", 16 * DC)
        P.wait("A", "g_x", 16 * DC)
        P.wait("T", "v_on", 1)
        for tb in range(NT):
            if xwait is not None:
                P.wait("V", xwait[0], xwait[1](tb))
            for dc in range(DC):
                par = dc % 2
                k = tb * DC + dc
                P.wait("V", "t_sx", max(0, P.val("t_sx") - 1))
                P.emit("V", lambda E, dc=dc, tb=tb, par=par: E.tensor_copy(
                    xb_t(par)[:], xsl(dc, tb * 512)), inc=("v_xb", 1))
                P.wait("A", "t_sx", max(0, P.val("t_sx") - 1))
                if xwait is not None:
                    P.wait("A", xwait[0], xwait[1](tb))
                P.emit("A", lambda E, par=par, dc=dc, tb=tb: E.activation(
                    xq_t(par)[:], xsl(dc, tb * 512), AF.Square), inc=("a_sq", 1))
                P.wait("T", "v_xb")
                P.emit("T", lambda E, par=par, dc=dc: E.matmul(
                    PB[6][0:1, :], ones[:, 0:1], xb_t(par)[:],
                    start=(dc == 0), stop=(dc == DC - 1)))
                P.wait("T", "a_sq")
                P.emit("T", lambda E, par=par, dc=dc: E.matmul(
                    PB[7][0:1, :], ones[:, 0:1], xq_t(par)[:],
                    start=(dc == 0), stop=(dc == DC - 1)),
                    inc=("t_sx", 1))
            P.wait("V", "t_sx")
            P.wait("V", "s_sg")
            P.emit("V", lambda E: E.tensor_copy(stg[0:1, 0:512], PB[6][0:1, :]))
            P.drain("V")
            P.emit("V", lambda E: E.tensor_copy(stg[0:1, 512:1024], PB[7][0:1, :]),
                   inc=("v_sg", 1))
            P.wait("S", "v_sg")
            P.emit("S", lambda E, tb=tb: E.dma_start(
                out=mu_d[0:1, tb * 512:tb * 512 + 512], in_=stg[0:1, 0:512]),
                inc=("s_sg", 16))
            P.emit("S", lambda E, tb=tb: E.dma_start(
                out=m2_d[0:1, tb * 512:tb * 512 + 512], in_=stg[0:1, 512:1024]),
                inc=("s_sg", 16))
            P.wait("T", "v_sg")

        P.wait("S", "s_sg")
        P.emit("S", lambda E: E.dma_start(
            out=foldv(0), in_=mu_d.rearrange("o (p n) -> (o p) n", p=128)),
            inc=("s_fl", 16))
        P.emit("S", lambda E: E.dma_start(
            out=foldv(1), in_=m2_d.rearrange("o (p n) -> (o p) n", p=128)),
            inc=("s_fl", 16))
        P.wait("V", "s_fl")
        P.emit("V", lambda E: E.tensor_mul(foldv(2), foldv(0), foldv(0)))
        P.drain("V")
        P.emit("V", lambda E: E.tensor_sub(foldv(2), foldv(1), foldv(2)))
        P.drain("V")
        P.emit("V", lambda E: E.tensor_scalar_add(foldv(3), foldv(2), EPS),
               inc=("v_u", 1))
        P.wait("A", "v_u")
        P.emit("A", lambda E: E.activation(foldv(4), foldv(3), AF.Sqrt),
               inc=("a_s0", 1))
        P.wait("V", "a_s0")
        P.emit("V", lambda E: E.reciprocal(foldv(5), foldv(4)))
        P.drain("V")
        P.emit("V", lambda E: E.tensor_mul(foldv(6), foldv(5), foldv(5)))
        P.drain("V")
        P.emit("V", lambda E: E.tensor_mul(foldv(6), foldv(6), foldv(3)))
        P.drain("V")
        P.emit("V", lambda E: E.tensor_scalar(foldv(6), foldv(6), -0.5, 1.5,
                                              ALU.mult, ALU.add))
        P.drain("V")
        P.emit("V", lambda E: E.tensor_mul(foldv(6), foldv(6), foldv(5)),
               inc=("v_r", 1))
        P.wait("S", "v_r")
        P.emit("S", lambda E: E.dma_start(
            out=r_d.rearrange("o (p n) -> (o p) n", p=128), in_=foldv(6)),
            inc=("s_r", 16))
        if with_t2:
            P.drain("V")
            P.emit("V", lambda E: E.reciprocal(foldv(2), foldv(3)))
            P.drain("V")
            P.emit("V", lambda E: E.tensor_scalar(foldv(2), foldv(2), -EPS,
                                                  1.0 + EPS, ALU.mult, ALU.add),
                   inc=("v_w", 1))
            P.wait("A", "v_w")
            P.emit("A", lambda E: E.activation(foldv(4), foldv(2), AF.Sqrt),
                   inc=("a_s1", 1))
            P.wait("V", "a_s1")
            P.emit("V", lambda E: E.reciprocal(foldv(5), foldv(4)))
            P.drain("V")
            P.emit("V", lambda E: E.tensor_mul(foldv(7), foldv(5), foldv(5)))
            P.drain("V")
            P.emit("V", lambda E: E.tensor_mul(foldv(7), foldv(7), foldv(2)))
            P.drain("V")
            P.emit("V", lambda E: E.tensor_scalar(foldv(7), foldv(7), -0.5, 1.5,
                                                  ALU.mult, ALU.add))
            P.drain("V")
            P.emit("V", lambda E: E.tensor_mul(foldv(7), foldv(7), foldv(5)))
            P.drain("V")
            P.emit("V", lambda E: E.tensor_scalar_mul(foldv(7), foldv(7), 0.125))
            P.drain("V")
            P.emit("V", lambda E: E.tensor_mul(foldv(7), foldv(7), foldv(6)),
                   inc=("v_t2", 1))
            P.wait("S", "v_t2")
            P.emit("S", lambda E: E.dma_start(
                out=t2_d.rearrange("o (p n) -> (o p) n", p=128), in_=foldv(7)),
                inc=("s_t2", 16))

    def ffn(tag, store_out=False):
        ln_stats(tag, with_t2=False,
                 xwait=("v_wo", lambda tb: (tb + 1) * DC) if tag == "f2" else None)
        tg0 = P.val("t_g")
        tye0 = P.val("t_ye")
        tyd0 = P.val("t_yd")
        vyc0 = P.val("v_yc")
        vsp0 = P.val("v_sp")

        def emit_spike(tb, dc):
            if dc == 0:
                P.wait("V", "t_ye", tye0 + (tb + 1) * DC)
            ysrc = (lambda: ysb[:, dc * 512:(dc + 1) * 512]) if dc < 2 \
                else (lambda: PB[dc][:])
            P.emit("V", lambda E, dc=dc: E.tensor_single_scalar(
                acc[:], ysrc(), THETA[0], ALU.is_ge))
            P.drain("V")
            for i in range(1, 5):
                P.emit("V", lambda E, dc=dc, i=i: E.scalar_tensor_tensor(
                    acc[:], ysrc(), THETA[i], acc[:], ALU.is_ge, ALU.add))
                P.drain("V")
            P.emit("V", lambda E, dc=dc, tb=tb: E.scalar_tensor_tensor(
                xsl(dc, tb * 512), acc[:], 0.1, xsl(dc, tb * 512),
                ALU.mult, ALU.add), inc=("v_sp", 1))
            P.drain("V")
            if store_out:
                P.wait("S", "v_sp")
                P.emit("S", lambda E, dc=dc, tb=tb: E.dma_start(
                    out=xout[dc * 128:(dc + 1) * 128, tb * 512:(tb + 1) * 512],
                    in_=xsl(dc, tb * 512)), inc=("s_out", 16))

        def emit_apply(tb):
            P.wait("S", "v_ap")
            P.emit("S", lambda E, tb=tb: E.dma_start(
                out=MU_t[:], in_=bass.AP(mu_d.tensor, tb * 512, [[0, 128], [1, 512]])),
                inc=("s_mu", 16))
            P.wait("S", "s_r")
            P.emit("S", lambda E, tb=tb: E.dma_start(
                out=R_t[:], in_=bass.AP(r_d.tensor, tb * 512, [[0, 128], [1, 512]])),
                inc=("s_mu", 16))
            P.wait("V", "s_mu")
            P.wait("V", "t_g", tg0 + tb * FCN)
            for dc in range(DC):
                P.emit("V", lambda E, dc=dc, tb=tb: E.tensor_sub(
                    acc[:], xsl(dc, tb * 512), MU_t[:]))
                P.drain("V")
                P.emit("V", lambda E, dc=dc: E.tensor_mul(
                    xn_t(dc)[:], acc[:], R_t[:]), inc=("v_ap", 1))
                P.drain("V")

        emit_apply(0)
        for tb in range(NT):
            if tag == "f1" and tb == 1:
                P.wait("A", "g_wfc", 16 * DC)
                P.wait("A", "g_wfcb", 16 * DC)
                for dc in range(DC):
                    P.emit("A", lambda E, dc=dc: E.dma_start(
                        out=wf16[dc * 128:(dc + 1) * 128, :],
                        in_=WB[:, dc * FF: dc * FF + FF]), inc=("s_wf16", 16))
            vap_tb = P.val("v_ap") - DC
            P.wait("T", "v_yc", vyc0 + tb)
            if tag == "f2":
                P.wait("T", "g_wfc2", 16 * DC)
            for fc in range(FCN):
                par = fc % 2
                if tag == "f1" and tb == 0 and fc in (0, 16):
                    P.wait("T", "g_wfc" if fc == 0 else "g_wfcb", 16 * DC)
                P.wait("T", "a_rl", max(0, P.val("a_rl") - 1))
                for dc in range(DC):
                    if fc == 0:
                        P.wait("T", "v_ap", vap_tb + dc + 1)
                    P.emit("T", lambda E, par=par, dc=dc, fc=fc: E.matmul(
                        PB[par][:], wfc_t(dc, fc), xn_t(dc)[:],
                        start=(dc == 0), stop=(dc == DC - 1)),
                        inc=("t_g", 1) if dc == DC - 1 else None)
                P.wait("A", "t_g")
                P.wait("A", "v_h", max(0, P.val("v_h") - 1))
                P.emit("A", lambda E, par=par: E.activation(
                    hh_t(par)[:], PB[par][:], AF.Relu), inc=("a_rl", 1))
                P.wait("V", "a_rl")
                P.emit("V", lambda E, par=par, fc=fc: E.tensor_mul(
                    h_t(fc), hh_t(par)[:], hh_t(par)[:]), inc=("v_h", 1))
                if tb > 0 and fc % 4 == 3:
                    emit_spike(tb - 1, fc // 4)
            for e in range(8):
                par = e % 2
                ge = tb * 8 + e
                P.wait("G", "t_ye", max(tye0, tye0 + ge - 1))
                if tag == "f1" and tb == 0 and e >= 2:
                    P.wait("G", f"s_wp16p{e % 2}", 16 * (e // 2))
                if tag == "f2" and tb == 0 and e < 2:
                    P.wait("G", "t_s")
                if tag == "f1" and tb == 0:
                    for fl in range(4):
                        P.emit("G", lambda E, e=e, par=par, fl=fl: E.dma_start(
                            out=Q3[:, par * 4096 + fl * 1024: par * 4096 + fl * 1024 + 1024],
                            in_=wprojT[e * 512 + fl * 128: e * 512 + fl * 128 + 128, :]),
                            inc=(f"g_wp{par}", 16))
                    P.wait("S", f"g_wp{par}")
                    P.emit("S", lambda E, e=e, par=par: E.dma_start(
                        out=wp16[e * 128:(e + 1) * 128, :], in_=wpe_full(par)),
                        inc=(f"s_wp16p{par}", 16))
                else:
                    if tag == "f1" and tb == 1 and e < 2:
                        P.wait("G", f"s_wp16p{e % 2}", 64)
                    P.emit("G", lambda E, e=e, par=par: E.dma_start(
                        out=wpe_full(par), in_=wp16[e * 128:(e + 1) * 128, :]),
                        inc=(f"g_wp{par}", 16))
                P.wait("T", f"g_wp{par}")
                if e == 0:
                    P.wait("T", "v_h")
                for dc in range(DC):
                    if e == 0 and tb > 0:
                        P.wait("T", "v_sp", vsp0 + (tb - 1) * DC + dc + 1)
                    for fcl in range(4):
                        fc = e * 4 + fcl
                        st = (e == 0 and fcl == 0)
                        sp = (e == 7 and fcl == 3)
                        if dc == DC - 1 and fcl == 3:
                            mark = ("t_ye", 1)
                        elif e == 7 and fcl == 3 and dc < 2:
                            mark = ("t_yd", 1)
                        else:
                            mark = None
                        P.emit("T", lambda E, dc=dc, par=par, fcl=fcl, fc=fc,
                               st=st, sp=sp: E.matmul(
                            PB[dc][:], wpe_t(par, fcl, dc), h_t(fc),
                            start=st, stop=sp), inc=mark)
            if tb + 1 < NT:
                emit_apply(tb + 1)
            P.wait("V", "t_yd", tyd0 + (tb + 1) * 2)
            P.emit("V", lambda E: E.tensor_copy(ysb[:, 0:512], PB[0][:]))
            P.drain("V")
            P.emit("V", lambda E: E.tensor_copy(ysb[:, 512:1024], PB[1][:]),
                   inc=("v_yc", 1))
            P.drain("V")
            if tb == NT - 1:
                for dc in range(DC):
                    emit_spike(tb, dc)

    def attention():
        ln_stats("a", with_t2=True)
        P.wait("G", "t_g")
        P.wait("G", "t_ye")
        P.wait("G", "s_wf16", 16 * DC)
        for dc in range(DC):
            P.emit("G", lambda E, dc=dc: E.dma_start(
                out=WB[:, WQP + dc * 1024: WQP + dc * 1024 + 1024],
                in_=wq[dc * 128:(dc + 1) * 128, :]), inc=("g_wq", 16))
        for dc in range(DC):
            P.emit("G", lambda E, dc=dc: E.dma_start(
                out=WB[:, WKP + dc * 128: WKP + dc * 128 + 128],
                in_=wkk[dc * 128:(dc + 1) * 128, :]), inc=("g_wk", 16))
            P.emit("G", lambda E, dc=dc: E.dma_start(
                out=WB[:, WVP + dc * 64: WVP + dc * 64 + 64],
                in_=wvv[dc * 128:(dc + 1) * 128, :]), inc=("g_wk", 16))
        P.wait("V", "g_wq", 16 * DC)
        P.wait("V", "g_misc", 16)
        for dc in range(DC):
            P.emit("V", lambda E, dc=dc: E.tensor_scalar_mul(
                WB[:, WQP + dc * 1024: WQP + dc * 1024 + 1024],
                WB[:, WQP + dc * 1024: WQP + dc * 1024 + 1024],
                gam[:, dc:dc + 1]), inc=("v_gw", 1))
            P.drain("V")

        P.wait("V", "s_wp16p0", P.val("s_wp16p0"))
        P.wait("V", "s_wp16p1", P.val("s_wp16p1"))
        vapa0 = P.val("v_apa")
        for tb in range(NT):
            P.wait("S", "v_apa")
            P.wait("S", "v_qs")
            P.emit("S", lambda E, tb=tb: E.dma_start(
                out=MU_t[:], in_=bass.AP(mu_d.tensor, tb * 512, [[0, 128], [1, 512]])),
                inc=("s_muo", 16))
            P.wait("S", "s_r")
            P.emit("S", lambda E, tb=tb: E.dma_start(
                out=R_t[:], in_=bass.AP(r_d.tensor, tb * 512, [[0, 128], [1, 512]])),
                inc=("s_mua", 16))
            P.wait("S", "s_t2")
            P.emit("S", lambda E, tb=tb: E.dma_start(
                out=T2_t, in_=bass.AP(t2_d.tensor, tb * 512, [[0, 128], [1, 512]])),
                inc=("s_mua", 16))
            P.wait("V", "s_muo", 16 * (tb + 1))
            P.wait("V", "v_gw", DC)
            P.wait("V", "t_kv", tb)
            P.wait("V", "t_vt", tb)
            P.wait("V", "t_qs", tb * DC)
            for dc in range(DC):
                P.emit("V", lambda E, dc=dc, tb=tb: E.tensor_sub(
                    a_t(dc)[:], xsl(dc, tb * 512), MU_t[:]), inc=("v_apa", 1))
                P.drain("V")
            P.wait("T", "g_wk", 16 * 2 * DC)
            P.wait("T", "v_kv", tb)
            for dc in range(DC):
                P.wait("T", "v_apa", vapa0 + tb * DC + dc + 1)
                P.emit("T", lambda E, dc=dc: E.matmul(
                    PB[3][:], wkk_t(dc), a_t(dc)[:],
                    start=(dc == 0), stop=(dc == DC - 1)),
                    inc=("t_kv", 1) if dc == DC - 1 else None)
            P.wait("T", "v_vt", tb)
            for dc in range(DC):
                P.emit("T", lambda E, dc=dc: E.matmul(
                    PB[4][0:64, :], wvv_t(dc), a_t(dc)[:],
                    start=(dc == 0), stop=(dc == DC - 1)),
                    inc=("t_vt", 1) if dc == DC - 1 else None)
            P.wait("V", "s_mua")
            P.wait("V", "t_kv")
            P.emit("V", lambda E, tb=tb: E.tensor_mul(
                kt2_t(tb * 512), PB[3][:], R_t[:]), inc=("v_kv", 1))
            P.drain("V")
            P.wait("V", "t_vt")
            P.emit("V", lambda E, tb=tb: E.tensor_mul(
                vt_t(tb * 512, 512), PB[4][0:64, :], R_t[0:64, :]),
                inc=("v_vt", 1))
            P.drain("V")
            for oc in range(DC):
                bank = 5 + (oc % 2)
                P.wait("T", "v_qs", max(0, P.val("v_qs") - 1))
                for dc in range(DC):
                    P.emit("T", lambda E, oc=oc, dc=dc, bank=bank: E.matmul(
                        PB[bank][:], wq_t(dc, oc), a_t(dc)[:],
                        start=(dc == 0), stop=(dc == DC - 1)),
                        inc=("t_qs", 1) if dc == DC - 1 else None)
                P.wait("V", "t_qs")
                P.emit("V", lambda E, oc=oc, tb=tb, bank=bank: E.tensor_mul(
                    qs_t(oc, tb * 512), PB[bank][:], T2_t), inc=("v_qs", 1))
                P.drain("V")
                P.emit("V", lambda E, oc=oc, tb=tb: E.tensor_mul(
                    WB[:, 8192:8704], qs_t(oc, tb * 512), qs_t(oc, tb * 512)),
                    inc=("v_qn2", 1))
                P.drain("V")
                P.wait("T", "v_qn2")
                P.wait("T", "v_qn")
                P.emit("T", lambda E: E.matmul(
                    PB[7][0:1, :], ones[:, 1:2], WB[:, 8192:8704],
                    start=True, stop=True), inc=("t_qn", 1))
                P.wait("V", "t_qn")
                P.emit("V", lambda E, oc=oc, tb=tb: E.reduce_max(
                    crow[0:1, oc * 4 + tb: oc * 4 + tb + 1], PB[7][0:1, :],
                    axis=mybir.AxisListType.X), inc=("v_qn", 1))
                P.drain("V")

        P.wait("S", "v_apa")
        P.emit("S", lambda E: E.dma_start(out=ident, in_=identd), inc=("s_tri", 16))
        P.wait("S", "v_vt")
        P.emit("S", lambda E: E.dma_start_transpose(
            WB[:, VBP:VBP + S // 2].rearrange("p (c n) -> p c n", n=64),
            WB[0:64, VTP:VTP + S].rearrange("p (c n) -> p c n", n=128)),
            inc=("s_vb", 16))

        P.wait("A", "t_qn")
        P.wait("A", "s_vb", 16)
        P.wait("T", "v_qn")
        for tt in range(NT):
            P.wait("A", "v_kv")
            P.wait("A", "t_kn")
            P.emit("A", lambda E, tt=tt: E.activation(
                scr_t(512, 0, 64), kt2_t(tt * 512, 512, 0, 64), AF.Square),
                inc=("a_kn", 1))
            P.wait("T", "a_kn")
            P.wait("T", "v_kn")
            P.emit("T", lambda E: E.matmul(
                PB[7][0:1, :], ones[0:64, 1:2], scr_t(512, 0, 64),
                start=True, stop=True), inc=("t_kn", 1))
            P.wait("V", "t_kn")
            P.emit("V", lambda E, tt=tt: E.reduce_max(
                crow[0:1, 32 + tt: 33 + tt], PB[7][0:1, :],
                axis=mybir.AxisListType.X), inc=("v_kn", 1))
        P.wait("V", "v_qn")
        P.wait("V", "v_kn")
        for pc in range(NP):
            P.emit("V", lambda E, pc=pc: E.reduce_max(
                crow[0:1, 40 + pc: 41 + pc], crow[0:1, pc * 4: pc * 4 + NT],
                axis=mybir.AxisListType.X))
            P.drain("V")
        P.emit("V", lambda E: E.reduce_max(
            crow[0:1, 48:49], crow[0:1, 32: 32 + NT], axis=mybir.AxisListType.X))
        P.drain("V")
        P.emit("V", lambda E: E.tensor_single_scalar(
            crow[0:1, 40:48], crow[0:1, 40:48], crow[0:1, 48:49], ALU.mult),
            inc=("v_c1", 1))
        P.wait("A", "v_c1")
        P.emit("A", lambda E: E.activation(
            crow[0:1, 56:64], crow[0:1, 40:48], AF.Sqrt), inc=("a_c", 1))
        P.wait("V", "a_c")
        P.emit("V", lambda E: E.tensor_scalar_mul(
            crow[0:1, 56:64], crow[0:1, 56:64], -1.0), inc=("v_c2", 1))
        P.wait("S", "v_c2")
        P.emit("S", lambda E: E.dma_start(out=nc_d, in_=crow[0:1, 56:64]),
               inc=("s_c", 16))
        P.wait("S", "s_c")
        P.emit("S", lambda E: E.dma_start(
            out=ncb[:], in_=bass.AP(nc_d.tensor, 0, [[0, 128], [1, 8]])),
            inc=("s_c", 16))

        P.wait("T", "s_vb", 16)
        P.wait("T", "v_kv")
        P.wait("T", "v_vt")
        P.wait("T", "v_qs")
        P.wait("A", "s_c", 32)
        P.wait("A", "s_tri", 16)
        P.wait("T", "s_tri", 16)
        P.wait("V", "g_tri", 16)
        P.wait("S", "v_gw", DC)
        P.emit("G", lambda E: E.dma_start(out=e2[:], in_=e2d), inc=("g_e2", 16))
        P.wait("T", "g_e2", 16)

        NQT = S // 128
        tr_done = {}
        ex_done = {}
        unit_of = {}
        uidx = 0
        pend_av = None
        pend_msc = None

        def emit_av(qt, jb, u):
            ext = (qt + 1) * 128
            jbn = (ext + 511) // 512
            N = min(512, ext - jb * 512)
            jcn = N // 128
            ob = 4 if qt % 2 == 0 else 6
            P.wait("T", "s_tr", tr_done[u])
            P.wait("T", "t_av", u)
            if jb == 0:
                P.wait("T", "v_msc", max(0, 2 * (qt - 1)))
            for jcl in range(jcn):
                jc = jb * 4 + jcl
                st = (jb == 0 and jcl == 0)
                sp = (jb == jbn - 1 and jcl == jcn - 1)
                for hl in range(2):
                    for half in range(2):
                        lastmm = (jcl == jcn - 1 and hl == 1 and half == 1)
                        off = ((half * 8 + hl) * jcn + jcl) * 128
                        rhs = bass.AP(WB, off, [[WB.shape[1], 128],
                                                [2 * jcn * 128, 4], [1, 128]])
                        P.emit("T", lambda E, jc=jc, rhs=rhs, hl=hl, half=half,
                               st=st, sp=sp, ob=ob: E.matmul(
                                   PB[ob + half][hl * 64:(hl + 1) * 64, 0:512],
                                   vb_t(jc), rhs,
                                   start=st, stop=sp, skip_group_check=True,
                                   tile_position=(0, hl * 64)),
                               inc=("t_av", 1) if lastmm else None)

        def emit_msc(qt):
            ob = 4 if qt % 2 == 0 else 6
            spb = 2 * (1 - (P.val("t_s") // 2) % 2)
            P.wait("V", "a_ex", ex_done[qt])
            po = (qt % 2) * 64
            P.emit("V", lambda E, po=po: E.tensor_reduce(
                rs16[:], sacc[:, po:po + 64].rearrange("p (g j) -> p g j", j=4),
                axis=mybir.AxisListType.X, op=ALU.add))
            P.drain("V")
            P.emit("V", lambda E: E.reciprocal(rs16[:], rs16[:]),
                   inc=("v_rs", 1))
            P.drain("V")
            P.wait("T", "v_rs", qt + 1)
            P.wait("T", "a_ex")
            for pc in range(NP):
                P.emit("T", lambda E, pc=pc, mb=spb: E.transpose(
                    PB[mb + pc // 4][0:2, (pc % 4) * 128:(pc % 4) * 128 + 128],
                    rs16[:, pc * 2: pc * 2 + 2], ident),
                    inc=("t_mst", 1) if pc == NP - 1 else None)
            P.wait("V", "t_mst", qt + 1)
            P.emit("V", lambda E, mb=spb: E.tensor_copy(
                r2_t(0, 512), PB[mb][0:2, 0:512]))
            P.drain("V")
            P.emit("V", lambda E, mb=spb: E.tensor_copy(
                r2_t(512, 512), PB[mb + 1][0:2, 0:512]), inc=("v_r2", 1))
            P.wait("T", "v_r2", qt + 1)
            for half in range(2):
                P.emit("T", lambda E, half=half, mb=spb: E.matmul(
                    PB[mb + half][:, 0:512], e2[:], r2_t(half * 512, 512),
                    start=True, stop=True), inc=("t_msm", 1) if half else None)
            P.wait("V", "t_msm", qt + 1)
            for half in range(2):
                P.emit("V", lambda E, half=half, mb=spb: E.tensor_copy(
                    msc_t(half * 512, 512), PB[mb + half][:, 0:512]))
                P.drain("V")
            P.emit("V", lambda E: E.engine_nop(), inc=("v_mc", 1))
            P.wait("V", "t_av", unit_of[(qt, ((qt + 1) * 128 + 511) // 512 - 1)] + 1)
            for half in range(2):
                dstbase = half * 4 * S + qt * 128
                dst3 = bass.AP(HB, dstbase, [[HB.shape[1], 128], [S, 4], [1, 128]])
                P.emit("V", lambda E, half=half, ob=ob, dst3=dst3: E.tensor_tensor(
                    dst3,
                    PB[ob + half][:, 0:512].rearrange("p (pc n) -> p pc n", n=128),
                    msc_t(half * 512, 512).rearrange("p (pc n) -> p pc n", n=128),
                    op=ALU.mult), inc=("v_msc", 1))
                P.drain("V")

        for qt in range(NQT):
            ext = (qt + 1) * 128
            jbn = (ext + 511) // 512
            par = qt % 2
            po = par * 64
            P.wait("V", "a_ex")
            P.emit("V", lambda E, po=po: E.memset(sacc[:, po:po + 64], 0.0),
                   inc=("v_sz", 1))
            for jb in range(jbn):
                u = uidx
                uidx += 1
                unit_of[(qt, jb)] = u
                upar = u % 2
                N = min(512, ext - jb * 512)
                for pc in range(NP):
                    i = P.val("t_s") // 2
                    bA = (i % 2) * 2
                    bB = bA + 1
                    P.wait("T", "a_ex", max(0, P.val("a_ex") - 2))
                    P.wait("T", "v_mc")
                    P.emit("T", lambda E, pc=pc, qt=qt, jb=jb, N=N, bA=bA:
                           E.matmul(PB[bA][:, 0:N],
                                    qs_t(pc, qt * 128, 128, 0, 64),
                                    kt2_t(jb * 512, N, 0, 64),
                                    start=True, stop=True))
                    P.emit("T", lambda E, pc=pc, qt=qt, jb=jb, N=N, bB=bB:
                           E.matmul(PB[bB][:, 0:N],
                                    qs_t(pc, qt * 128, 128, 64, 64),
                                    kt2_t(jb * 512, N, 64, 64),
                                    start=True, stop=True), inc=("t_s", 2))
                    diag = qt * 128 - jb * 512
                    if 0 <= diag < N:
                        P.wait("V", "t_s")
                        for b in (bA, bB):
                            P.emit("V", lambda E, b=b, diag=diag: E.tensor_add(
                                PB[b][:, diag:diag + 128],
                                PB[b][:, diag:diag + 128], tri[:]),
                                inc=("v_msk", 1))
                            P.drain("V")
                    if 0 <= diag < N:
                        P.wait("A", "v_msk")
                    P.wait("A", "t_s")
                    if jb == 0 and pc == 0:
                        P.wait("A", "v_sz")
                    if u >= 2 and pc == 0:
                        P.wait("A", "s_tr", tr_done[u - 2])
                    for hl in range(2):
                        g = pc * 2 + hl
                        P.emit("A", lambda E, hl=hl, par=upar, jb=jb, N=N,
                               b=(bA, bB)[hl], pc=pc, g=g, po=po: E.activation(
                                   p2_t(par, pc, hl, N), PB[b][:, 0:N],
                                   AF.Exp, bias=ncb[:, pc:pc + 1], scale=1.0,
                                   accum_out=sacc[:, po + g * 4 + jb:
                                                  po + g * 4 + jb + 1]),
                               inc=("a_ex", 1))
                P.wait("S", "a_ex")
                P.wait("S", "t_av", u)
                jcn = N // 128
                srcbase = P0P if upar == 0 else P1P
                src2 = bass.AP(WB, srcbase, [[WB.shape[1], 128], [1, 16 * N]])
                dst3 = bass.AP(WB, 0,
                               [[WB.shape[1], 128], [128, 16 * jcn], [1, 128]])
                P.emit("S", lambda E, dst3=dst3, src2=src2:
                       E.dma_start_transpose(dst3, src2), inc=("s_tr", 16))
                tr_done[u] = P.val("s_tr")
                if pend_av is not None:
                    pqt, pjb, pu = pend_av
                    emit_av(pqt, pjb, pu)
                    pext = (pqt + 1) * 128
                    if pjb == (pext + 511) // 512 - 1:
                        emit_msc(pqt)
                pend_av = (qt, jb, u)
            ex_done[qt] = P.val("a_ex")
        pqt, pjb, pu = pend_av
        emit_av(pqt, pjb, pu)
        emit_msc(pqt)

        P.wait("G", "a_ex")
        P.wait("G", "s_tr")
        for pc in range(NP):
            P.emit("G", lambda E, pc=pc: E.dma_start(
                out=WB[:, WOP + pc * 1024: WOP + pc * 1024 + 1024],
                in_=wo[pc * 128:(pc + 1) * 128, :]), inc=("g_wo", 16))
        if stage == "attn":
            otd = nc.dram_tensor("otd", [128, 16384], BF16, kind="ExternalOutput").ap()
            P.wait("S", "v_msc")
            P.emit("S", lambda E: E.dma_start(out=otd, in_=HB[:]), inc=("s_out", 16))
        P.wait("T", "v_msc")
        P.wait("T", "g_wo", 16 * NP)
        for tt in range(NT):
            for dc in range(DC):
                bank = (tt * DC + dc) % 2
                P.wait("T", "v_wo", max(0, P.val("v_wo") - 1))
                for pc in range(NP):
                    P.emit("T", lambda E, pc=pc, dc=dc, tt=tt, bank=bank: E.matmul(
                        PB[bank][:], wo_t(pc, dc), outT_t(pc, tt * 512, 512),
                        start=(pc == 0), stop=(pc == NP - 1)),
                        inc=("t_wo", 1) if pc == NP - 1 else None)
                P.wait("V", "t_wo")
                P.emit("V", lambda E, dc=dc, tt=tt, bank=bank: E.tensor_add(
                    xsl(dc, tt * 512), PB[bank][:], xsl(dc, tt * 512)),
                    inc=("v_wo", 1))
                P.drain("V")

        P.wait("G", "t_av")
        for dc in (0, 1, 4, 5):
            P.emit("G", lambda E, dc=dc: E.dma_start(
                out=WB[:, dc * FF: dc * FF + FF], in_=wf16[dc * 128:(dc + 1) * 128, :]),
                inc=("g_wfc2", 16))
        P.wait("G", "v_msc")
        P.wait("G", "t_msm")
        for dc in (6, 7):
            P.emit("G", lambda E, dc=dc: E.dma_start(
                out=WB[:, dc * FF: dc * FF + FF], in_=wf16[dc * 128:(dc + 1) * 128, :]),
                inc=("g_wfc2", 16))
        P.wait("G", "t_wo")
        for dc in (2, 3):
            P.emit("G", lambda E, dc=dc: E.dma_start(
                out=WB[:, dc * FF: dc * FF + FF], in_=wf16[dc * 128:(dc + 1) * 128, :]),
                inc=("g_wfc2", 16))


    ffn("f1", store_out=(stage == "ffn1"))
    if stage != "ffn1":
        attention()
        if stage == "attn":
            P.wait("S", "v_wo")
            for dc in range(DC):
                P.emit("S", lambda E, dc=dc: E.dma_start(
                    out=xout[dc * 128:(dc + 1) * 128, :], in_=xsl(dc, 0, S)),
                    inc=("s_out", 16))
        else:
            for e in ("V", "A", "T", "G"):
                P.wait(e, "t_s")
                P.wait(e, "t_av")
                P.wait(e, "a_ex")
                P.wait(e, "t_mst")
            ffn("f2", store_out=True)
    P.wait("S", "s_out")
    for name in list(P.sems):
        if name.startswith("s_"):
            P.wait("S", name)
        elif name.startswith("g_"):
            P.wait("G", name)

    semnames = sorted(P.sems.keys())
    with contextlib.ExitStack() as stack:
        sems = {n: stack.enter_context(nc.semaphore(n)) for n in semnames}
        with nc.Block() as block:
            @block.tensor
            def _(E):
                for fn in P.streams["T"]:
                    fn(E, sems)

            @block.vector
            def _(E):
                for fn in P.streams["V"]:
                    fn(E, sems)

            @block.scalar
            def _(E):
                for fn in P.streams["A"]:
                    fn(E, sems)

            @block.sync
            def _(E):
                for fn in P.streams["S"]:
                    fn(E, sems)

            @block.gpsimd
            def _(E):
                for fn in P.streams["G"]:
                    fn(E, sems)
    return nc


_cache = {}


def _get_program(S=2048, stage="full"):
    key = (S, stage)
    if key not in _cache:
        _cache[key] = build_program(S, stage)
    return _cache[key]


def make_inputs(L0_attn_x, L0_attn_gamma, L0_attn_Wq, L0_attn_Wkv, L0_attn_Wo,
                L0_ffn_W_fc, L0_ffn_W_proj):
    B = L0_attn_x.shape[0]
    wfcT_h = np.ascontiguousarray(np.asarray(L0_ffn_W_fc, np.float32).T)
    wprojT_h = np.ascontiguousarray(np.asarray(L0_ffn_W_proj, np.float32).T)
    wq_h = np.ascontiguousarray(np.asarray(L0_attn_Wq, np.float32))
    wkv = np.asarray(L0_attn_Wkv, np.float32)
    wkk_h = np.ascontiguousarray(np.concatenate([wkv[:, 0:DH], wkv[:, 0:DH]], axis=1))
    wvv_h = np.ascontiguousarray(wkv[:, DH:2 * DH])
    wo_h = np.ascontiguousarray(np.asarray(L0_attn_Wo, np.float32))
    gam_h = np.ascontiguousarray(
        np.asarray(L0_attn_gamma, np.float32).reshape(DC, 128).T)
    tri_h = np.where(np.arange(128)[None, :] > np.arange(128)[:, None],
                     np.float32(NEGBIG), np.float32(0.0)).astype(np.float32)
    ident_h = np.eye(128, dtype=np.float32)
    e2_h = np.zeros((2, 128), np.float32)
    e2_h[0, 0:64] = 1.0
    e2_h[1, 64:128] = 1.0
    maps = []
    for b in range(B):
        xb = np.ascontiguousarray(np.asarray(L0_attn_x[b], np.float32).T)
        maps.append({
            "xin": xb, "wfcT": wfcT_h, "wprojT": wprojT_h, "wq": wq_h,
            "wkk": wkk_h, "wvv": wvv_h, "wo": wo_h, "gamd": gam_h,
            "trid": tri_h, "identd": ident_h, "e2d": e2_h,
        })
    return maps


def kernel(L0_attn_x, L0_attn_gamma, L0_attn_Wq, L0_attn_Wkv, L0_attn_Wo,
           L0_ffn_W_fc, L0_ffn_W_proj):
    B, S, _ = L0_attn_x.shape
    maps = make_inputs(L0_attn_x, L0_attn_gamma, L0_attn_Wq, L0_attn_Wkv,
                       L0_attn_Wo, L0_ffn_W_fc, L0_ffn_W_proj)
    nc_prog = _get_program(S, "full")
    res = run_bass_kernel_spmd(nc_prog, maps, core_ids=list(range(B)))
    out = np.stack([np.asarray(r["xout"], np.float32).T for r in res.results],
                   axis=0)
    return np.ascontiguousarray(out)
